# version 01
# baseline (speedup 1.0000x reference)
import numpy as np
import concourse.bass as bass
import concourse.mybir as mybir
from concourse.bass_utils import run_bass_kernel_spmd

F32 = mybir.dt.float32
BF16 = mybir.dt.bfloat16
AF = mybir.ActivationFunctionType
ALU = mybir.AluOpType

D = 1024
T = 2048
HC = 8
NT = 4
TC = 16
NH = 16
HD = 64
E = 64
I_EXP = 704
BIG = 1.0e30
LN_EPS = 1.0e-5
RSCALE = 2.5

UNITS = [dict(kind="e", e=e, i=I_EXP) for e in range(E)] + [
    dict(kind="sa", i=768), dict(kind="sb", i=640)]
for u in UNITS:
    u["mc"] = (u["i"] + 127) // 128
    u["lc"] = u["i"] - 128 * (u["mc"] - 1)

N_UNITS = 66
DO_FFN1 = True
DO_ATTN = True
DO_FFN2 = True
DEV_STAGE = ""
DEV_UPART = ""
DEV_APART = ""


def _ap(t, off, dims):
    return bass.AP(t, off, dims)


class _MockInstr:
    def then_inc(self, *a, **k):
        return self


class _MockEng:
    def wait_ge(self, *a, **k):
        return None

    def __getattr__(self, name):
        return lambda *a, **k: _MockInstr()


class _MockBlock:
    def __enter__(self):
        return self

    def __exit__(self, *a):
        return False


class P:

    def __init__(self, nc):
        self.nc = nc
        self.sem = {}
        for s in ["cast", "wload", "wg", "wu", "wd", "wrw", "store", "pe", "act", "dve", "pool"]:
            self.sem[s] = nc.semaphore(s).__enter__()
        self.marks = {}
        self.dry = True
        self.cnt = {}

    def start(self, dry):
        self.dry = dry
        self.cnt = {k: 0 for k in self.sem}

    def block(self):
        return _MockBlock() if self.dry else self.nc.Block()

    def on(self, blk, eng, fn):
        if self.dry:
            fn(_MockEng())
        else:
            getattr(blk, eng)(fn)

    def mark(self, label, s):
        if self.dry:
            self.marks[label] = self.cnt[s]

    def inc(self, instr, s, label=None, n=1):
        self.cnt[s] += n
        if not self.dry:
            instr.then_inc(self.sem[s], n)
        if label is not None:
            if self.dry:
                self.marks[label] = self.cnt[s]
            else:
                assert self.marks[label] == self.cnt[s], (label, self.marks[label], self.cnt[s])
        return self.cnt[s]

    def dinc(self, instr, s, label=None):
        self.cnt[s] += 1
        if not self.dry:
            instr.then_inc(self.sem[s], 16)
        if label is not None:
            if self.dry:
                self.marks[label] = self.cnt[s]
            else:
                assert self.marks[label] == self.cnt[s], label
        return self.cnt[s]

    STRICT = False

    def chain(self, eng, s, instr):
        c = self.inc(instr, s)
        if not self.dry:
            eng.wait_ge(self.sem[s], c)
        return c

    def wait(self, eng, s, label=None, val=None):
        if self.dry:
            return
        if val is None:
            if label is not None and label not in self.marks and P.STRICT:
                raise AssertionError(f"missing mark: {label}")
            val = self.marks.get(label, 0) if label is not None else self.cnt[s]
        if val <= 0:
            return
        eng.wait_ge(self.sem[s], val)

    def dwait(self, eng, s, label=None, val=None):
        if self.dry:
            return
        if val is None:
            if label is not None and label not in self.marks and P.STRICT:
                raise AssertionError(f"missing mark: {label}")
            val = self.marks.get(label, 0) if label is not None else self.cnt[s]
        if val <= 0:
            return
        eng.wait_ge(self.sem[s], 16 * val)


GCH = {}
DCH = {}
for _call in (0, 1):
    _gj = 0
    _dj = 0
    for _ui in range(len(UNITS)):
        _mc = UNITS[_ui]["mc"]
        for _nt in range(NT):
            for _m in range(_mc):
                GCH[(_call, _ui, _nt, _m)] = _gj
                _gj += 1
        for _nt in range(NT):
            for _hc in range(HC):
                DCH[(_call, _ui, _nt, _hc)] = _dj
                _dj += 1


def build_nc():
    nc = bass.Bass("TRN2", target_bir_lowering=False)

    def inpt(name, shape, dtype=F32):
        return nc.dram_tensor(name, shape, dtype, kind="ExternalInput").ap()

    x_in = inpt("x_fm", [D, T])
    wq = inpt("wq", [D, D]); wk = inpt("wk", [D, D])
    wv = inpt("wv", [D, D]); wo = inpt("wo", [D, D])
    qb_i = inpt("qb", [D]); kb_i = inpt("kb", [D]); vb_i = inpt("vb", [D]); ob_i = inpt("ob", [D])
    gate_w = inpt("gate_w", [E, D]); gate_b = inpt("gate_b", [E])
    Wg = inpt("Wg", [66, 768, D]); Wu = inpt("Wu", [66, 768, D]); Wd = inpt("Wd", [66, D, 768])
    cos_c = inpt("cos_t", [128, T], BF16)
    sin_c = inpt("sin_t", [128, T], BF16)
    perm_c = inpt("perm", [128, 128], BF16)
    mask_c = inpt("masks", [128, 4 * 512], BF16)
    iden_c = inpt("iden", [128, 128], F32)
    y_out = nc.dram_tensor("y_fm", [D, T], F32, kind="ExternalOutput").ap()
    if DEV_STAGE:
        xn_out = nc.dram_tensor("xn_out", [D, T], BF16, kind="ExternalOutput").ap()
        rw_out = nc.dram_tensor("rw_out", [E, T], F32, kind="ExternalOutput").ap()
        dn_out = nc.dram_tensor("dn_out", [128, 16], F32, kind="ExternalOutput").ap()
        rcp_out = nc.dram_tensor("rcp_out", [128, 16], F32, kind="ExternalOutput").ap()
        sco_out = nc.dram_tensor("sco_out", [128, 1024], F32, kind="ExternalOutput").ap()
        sel_out = nc.dram_tensor("sel_out", [128, 1024], F32, kind="ExternalOutput").ap()

    gbf, ubf, dbf = [], [], []
    for ui, u in enumerate(UNITS):
        gbf.append(nc.dram_tensor(f"gbf{ui}", [HC, 768, 128], BF16))
        ubf.append(nc.dram_tensor(f"ubf{ui}", [HC, 768, 128], BF16))
        dbf.append(nc.dram_tensor(f"dbf{ui}", [6, D, 128], BF16))
    gwbf = nc.dram_tensor("gwbf", [E, D], BF16)
    rwT_d = nc.dram_tensor("rwT_d", [E, T], F32)

    cur = [nc.sbuf_base]

    def alloc(name, shape, dtype, align=32):
        sz = shape[-1] * mybir.dt.size(dtype)
        cur[0] = (cur[0] + align - 1) // align * align
        t = nc.alloc_sbuf_tensor_at(name, shape, dtype, offset=cur[0])
        cur[0] += sz
        return t

    x_fm = [alloc(f"x{c}", [128, T], F32) for c in range(HC)]
    xn = [alloc(f"xn{c}", [128, T], BF16) for c in range(HC)]
    iden = alloc("iden_sb", [128, 128], F32)
    perm = alloc("perm_sb", [128, 128], BF16)
    gate_wT = alloc("gate_wT", [128, 8 * 64], BF16)
    gb_bc = alloc("gb_bc", [128, E], F32)
    qb_pp = alloc("qb_pp", [128, 8], F32)
    kb_pp = alloc("kb_pp", [128, 8], F32)
    ob_pp = alloc("ob_pp", [128, 8], F32)
    ones_col = alloc("ones_col", [128, 1], F32)
    ones_colb = alloc("ones_colb", [128, 1], BF16)
    pdum = alloc("pdum", [1, 1], F32)
    ones_row = alloc("ones_row", [1, 128], F32)
    ones_rowb = alloc("ones_rowb", [1, 128], BF16)
    persist_end = cur[0]

    cur[0] = persist_end
    wgT_k = [alloc(f"wgT{k}", [128, 768], BF16) for k in range(HC)]
    wuT_k = [alloc(f"wuT{k}", [128, 768], BF16) for k in range(HC)]
    wdT_i = [alloc(f"wdT{k}", [128, 1024], BF16) for k in range(6)]
    rw_bc = alloc("rw_bc", [128, T], F32)
    hact = [alloc(f"hact{c}", [128, 6 * 512], BF16) for c in range(2)]
    sg_st = [alloc(f"sg{i}", [128, 512], F32) for i in range(2)]
    sgu_st = [alloc(f"sgu{i}", [128, 512], F32) for i in range(2)]
    ffn_end = cur[0]

    cur[0] = persist_end
    qh = [alloc(f"qh{c}", [128, T], BF16) for c in range(4)]
    kh = [alloc(f"kh{c}", [128, T], BF16) for c in range(4)]
    v_tm = alloc("v_tm", [128, TC * 520], BF16)
    ctxh = [alloc(f"ctxh{c}", [128, T], BF16) for c in range(4)]
    whalf = alloc("whalf", [128, 4096], BF16)
    attnT = [alloc(f"attnT{i}", [128, 512], BF16) for i in range(2)]
    rope_t0 = attnT[0]
    rope_t1 = attnT[1]
    cos_sb = alloc("cos_sb", [128, T], BF16)
    sin_sb = alloc("sin_sb", [128, T], BF16)
    mask_sb = alloc("mask_sb", [128, 4 * 512], BF16)
    vb_row = alloc("vb_row", [1, D], BF16)
    v_st = alloc("v_st", [128, 512], F32)
    sqst = v_st
    nmax = alloc("nmax", [1, 16], F32)
    nb_sb = [alloc(f"nb{c}", [128, 1], F32) for c in range(4)]
    rs_sb = alloc("rs_sb", [1, 512], F32)
    rs_bf = vb_row
    ctx_st = v_st
    attn_end = cur[0]

    cur[0] = persist_end
    r_scores = alloc("r_scores", [128, 1024], F32)
    r_sfc = alloc("r_sfc", [128, 1024], F32)
    r_a = alloc("r_a", [128, 1024], F32)
    r_b = alloc("r_b", [128, 1024], F32)
    r_m1 = alloc("r_m1", [128, 128], F32)
    r_m2 = alloc("r_m2", [128, 128], F32)
    r_gs = alloc("r_gs", [128, 128], F32)
    r_gw = alloc("r_gw", [128, 128], F32)
    r_gm = alloc("r_gm", [128, 128], F32)
    r_mj = alloc("r_mj", [128, 16], F32)
    r_c = alloc("r_c", [128, 1024], F32)
    r_dn = alloc("r_dn", [128, 16], F32)
    rwT_sb = alloc("rwT_sb", [64, T], F32)
    ln_a = alloc("ln_a", [1, T], F32)
    ln_b = alloc("ln_b", [1, T], F32)
    ln_c = alloc("ln_c", [1, T], F32)
    ln_st = [alloc(f"lnst{i}", [128, 512], F32) for i in range(2)]
    xsq_st = [alloc(f"xsq{i}", [128, 512], F32) for i in range(8)]
    rout_end = cur[0]
    assert rout_end <= attn_end, (rout_end, attn_end)
    assert max(ffn_end, attn_end) <= nc.sbuf_top, (ffn_end, attn_end, nc.sbuf_top)

    ps = [nc.psum_tensor(f"ps{i}", [128, 512], F32).__enter__() for i in range(8)]

    p = P(nc)

    def emit_unit_casts(g, ui):
        sg_ = _ap(Wg.tensor, ui * 768 * D, [[128, HC], [D, 768], [1, 128]])
        su_ = _ap(Wu.tensor, ui * 768 * D, [[128, HC], [D, 768], [1, 128]])
        p.dinc(g.dma_start(_ap(gbf[ui], 0, [[768 * 128, HC], [128, 768], [1, 128]]), sg_), "cast")
        p.dinc(g.dma_start(_ap(ubf[ui], 0, [[768 * 128, HC], [128, 768], [1, 128]]), su_), "cast")
        sd_ = _ap(Wd.tensor, ui * D * 768, [[768, D], [128, 6], [1, 128]])
        p.dinc(g.dma_start(_ap(dbf[ui], 0, [[128, D], [D * 128, 6], [1, 128]]), sd_),
               "cast", label=f"cast_u{ui}")

    def prologue():
        with p.block() as blk:
            def sync_fn(sy):
                for c in range(HC):
                    p.dinc(sy.dma_start(x_fm[c][:], x_in[c * 128:(c + 1) * 128, :]), "wload")
                p.dinc(sy.dma_start(iden[:], iden_c), "wload")
                p.dinc(sy.dma_start(perm[:], perm_c), "wload")
                p.dinc(sy.dma_start(gb_bc[:], _ap(gate_b.tensor, 0, [[0, 128], [1, E]])), "wload")
                for oc in range(8):
                    p.dinc(sy.dma_start(qb_pp[:, oc:oc + 1], _ap(qb_i.tensor, oc * 128, [[1, 128], [1, 1]])), "wload")
                for oc in range(8):
                    p.dinc(sy.dma_start(kb_pp[:, oc:oc + 1], _ap(kb_i.tensor, oc * 128, [[1, 128], [1, 1]])), "wload")
                for oc in range(8):
                    p.dinc(sy.dma_start(ob_pp[:, oc:oc + 1], _ap(ob_i.tensor, oc * 128, [[1, 128], [1, 1]])), "wload")
                p.mark("prologue_loads", "wload")
            p.on(blk, "sync", sync_fn)

            def g_fn(g):
                g.memset(ones_col[:], 1.0)
                g.memset(ones_colb[:], 1.0)
                g.memset(ones_row[:], 1.0)
                i = g.memset(ones_rowb[:], 1.0)
                p.inc(i, "pool", label="ones_ready")
                if (DO_FFN1 or DO_FFN2) and DEV_STAGE != "ln":
                    p.dinc(g.dma_start(gwbf.ap(), gate_w), "cast", label="cast_gw")
                    if DEV_STAGE != "rt":
                        for ui in range(min(3, N_UNITS)):
                            emit_unit_casts(g, ui)
                    p.dwait(g, "cast", val=p.cnt["cast"])
                    i = g.memset(pdum[:], 0.0)
                    p.inc(i, "pool", label="cast_pro")
            p.on(blk, "gpsimd", g_fn)

    def ln_stage(call):
        tag = f"ln{call}"
        with p.block() as blk:
            def t_fn(t):
                p.dwait(t, "wload", label="prologue_loads")
                for nt in range(NT):
                    for hc in range(HC):
                        i = t.matmul(ps[6][0:1, :], ones_col[:], x_fm[hc][:, nt * 512:(nt + 1) * 512],
                                     start=(hc == 0), stop=(hc == HC - 1))
                    p.inc(i, "pe")
                    for hc in range(HC):
                        p.wait(t, "act", label=f"{tag}_sq{nt}_{hc}")
                        i = t.matmul(ps[7][0:1, :], ones_col[:], xsq_st[hc][:],
                                     start=(hc == 0), stop=(hc == HC - 1))
                    p.inc(i, "pe", label=f"{tag}_stats{nt}")
                    p.wait(t, "dve", label=f"{tag}_evac{nt}")
                p.wait(t, "dve", label=f"{tag}_rows")
                for nt in range(NT):
                    i = t.matmul(ps[6][:, :], ones_row[:], ln_c[0:1, nt * 512:(nt + 1) * 512],
                                 start=True, stop=True)
                    p.inc(i, "pe")
                    i = t.matmul(ps[7][:, :], ones_row[:], ln_a[0:1, nt * 512:(nt + 1) * 512],
                                 start=True, stop=True)
                    p.inc(i, "pe", label=f"{tag}_bc{nt}")
                    p.wait(t, "dve", label=f"{tag}_norm{nt}")

            def s_fn(s):
                p.dwait(s, "wload", label="prologue_loads")
                for nt in range(NT):
                    for hc in range(HC):
                        if nt > 0:
                            p.wait(s, "pe", label=f"{tag}_stats{nt - 1}")
                        i = s.square(xsq_st[hc][:], x_fm[hc][:, nt * 512:(nt + 1) * 512])
                        p.inc(i, "act", label=f"{tag}_sq{nt}_{hc}")
                p.wait(s, "dve", label=f"{tag}_var")
                p.chain(s, "act", s.activation(ln_c[:], ln_b[:], AF.Ln))
                i = s.activation(ln_c[:], ln_c[:], AF.Exp, scale=-0.5)
                p.inc(i, "act", label=f"{tag}_sd")

            def v_fn(v):
                for nt in range(NT):
                    p.wait(v, "pe", label=f"{tag}_stats{nt}")
                    p.chain(v, "dve", v.tensor_copy(ln_a[0:1, nt * 512:(nt + 1) * 512], ps[6][0:1, :]))
                    i = v.tensor_copy(ln_b[0:1, nt * 512:(nt + 1) * 512], ps[7][0:1, :])
                    p.inc(i, "dve", label=f"{tag}_evac{nt}")
                p.wait(v, "dve", val=p.cnt["dve"])
                p.chain(v, "dve", v.tensor_scalar(ln_a[:], ln_a[:], 1.0 / D, 0.0, ALU.mult, ALU.add))
                p.chain(v, "dve", v.tensor_scalar(ln_b[:], ln_b[:], 1.0 / D, 0.0, ALU.mult, ALU.add))
                p.chain(v, "dve", v.tensor_tensor(ln_c[:], ln_a[:], ln_a[:], ALU.mult))
                p.chain(v, "dve", v.tensor_tensor(ln_b[:], ln_b[:], ln_c[:], ALU.subtract))
                i = v.tensor_scalar(ln_b[:], ln_b[:], LN_EPS, 0.0, ALU.add, ALU.add)
                p.inc(i, "dve", label=f"{tag}_var")
                p.wait(v, "act", label=f"{tag}_sd")
                i = v.tensor_tensor(ln_a[:], ln_a[:], ln_c[:], ALU.mult)
                p.inc(i, "dve", label=f"{tag}_rows")
                p.wait(v, "dve", val=p.cnt["dve"])
                for nt in range(NT):
                    p.wait(v, "pe", label=f"{tag}_bc{nt}")
                    for hc in range(HC):
                        p.chain(v, "dve", v.tensor_tensor(ln_st[hc % 2][:],
                                                          x_fm[hc][:, nt * 512:(nt + 1) * 512],
                                                          ps[6][:], ALU.mult))
                        i = v.tensor_tensor(xn[hc][:, nt * 512:(nt + 1) * 512],
                                            ln_st[hc % 2][:], ps[7][:], ALU.subtract)
                    p.inc(i, "dve", label=f"{tag}_norm{nt}")

            p.on(blk, "tensor", t_fn)
            p.on(blk, "scalar", s_fn)
            p.on(blk, "vector", v_fn)

    def router_stage(call):
        tag = f"rt{call}"

        def sc(t_, dims, off=0):
            return _ap(t_, off, dims)

        with p.block() as blk:
            def t_fn(t):
                if call == 0:
                    p.dwait(t, "wload", label=f"{tag}_gwT")
                for tc in range(TC):
                    if tc >= 2:
                        p.wait(t, "act", label=f"{tag}_sig{tc - 2}")
                    for kc in range(HC):
                        i = t.matmul(ps[6 + tc % 2][0:128, 0:64], xn[kc][:, tc * 128:(tc + 1) * 128],
                                     gate_wT[:, kc * 64:(kc + 1) * 64],
                                     start=(kc == 0), stop=(kc == HC - 1))
                    p.inc(i, "pe", label=f"{tag}_lg{tc}")
                p.wait(t, "dve", label=f"{tag}_rw")
                for tc in range(TC):
                    if tc >= 2:
                        p.wait(t, "dve", label=f"{tag}_tev{tc - 2}")
                    i = t.transpose(ps[6 + tc % 2][0:64, 0:128], r_sfc[:, tc * 64:(tc + 1) * 64], iden[:])
                    p.inc(i, "pe", label=f"{tag}_tr{tc}")

            def s_fn(s):
                for tc in range(TC):
                    p.wait(s, "pe", label=f"{tag}_lg{tc}")
                    i = s.activation(r_scores[:, tc * 64:(tc + 1) * 64], ps[6 + tc % 2][0:128, 0:64],
                                     AF.Sigmoid)
                    p.inc(i, "act", label=f"{tag}_sig{tc}")
                if True:
                    p.wait(s, "dve", label=f"{tag}_dn")
                    p.chain(s, "act", s.activation(r_mj[:], r_dn[:], AF.Ln))
                    i = s.activation(r_mj[:], r_mj[:], AF.Exp, scale=-1.0)
                    p.inc(i, "act", label=f"{tag}_rcp")

            def v_fn(v):
                p.wait(v, "act", label=f"{tag}_sig{TC - 1}")
                p.chain(v, "dve", v.tensor_tensor(sc(r_sfc, [[1024, 128], [64, 16], [1, 64]]), sc(r_scores, [[1024, 128], [64, 16], [1, 64]]), sc(gb_bc, [[64, 128], [0, 16], [1, 64]]), ALU.add))
                p.chain(v, "dve", v.tensor_reduce(r_m1[:], sc(r_sfc, [[1024, 128], [8, 128], [1, 8]]), axis=mybir.AxisListType.X, op=ALU.max))
                p.chain(v, "dve", v.tensor_tensor(sc(r_a, [[1024, 128], [8, 128], [1, 8]]), sc(r_sfc, [[1024, 128], [8, 128], [1, 8]]), sc(r_m1, [[128, 128], [1, 128], [0, 8]]), ALU.is_equal))
                p.chain(v, "dve", v.scalar_tensor_tensor(r_b[:], r_a[:], -BIG, r_sfc[:], ALU.mult, ALU.add))
                p.chain(v, "dve", v.tensor_reduce(r_m2[:], sc(r_b, [[1024, 128], [8, 128], [1, 8]]), axis=mybir.AxisListType.X, op=ALU.max))
                p.chain(v, "dve", v.tensor_tensor(r_gs[:], r_m1[:], r_m2[:], ALU.add))
                p.chain(v, "dve", v.tensor_copy(r_gw[:], r_gs[:]))
                p.chain(v, "dve", v.memset(r_gm[:], 0.0))
                for j in range(4):
                    p.chain(v, "dve", v.tensor_reduce(r_mj[:], sc(r_gw, [[128, 128], [8, 16], [1, 8]]), axis=mybir.AxisListType.X, op=ALU.max))
                    p.chain(v, "dve", v.tensor_tensor(sc(r_m1, [[128, 128], [8, 16], [1, 8]]), sc(r_gw, [[128, 128], [8, 16], [1, 8]]), sc(r_mj, [[16, 128], [1, 16], [0, 8]]), ALU.is_equal))
                    p.chain(v, "dve", v.tensor_tensor(r_gm[:], r_gm[:], r_m1[:], ALU.add))
                    p.chain(v, "dve", v.scalar_tensor_tensor(r_gw[:], r_m1[:], -BIG, r_gw[:], ALU.mult, ALU.add))
                p.chain(v, "dve", v.tensor_tensor(sc(r_b, [[1024, 128], [64, 16], [8, 8], [1, 8]]), sc(r_sfc, [[1024, 128], [64, 16], [8, 8], [1, 8]]), sc(r_gm, [[128, 128], [8, 16], [1, 8], [0, 8]]), ALU.mult))
                p.chain(v, "dve", v.memset(r_a[:], 0.0))
                p.chain(v, "dve", v.memset(r_dn[:], 0.0))
                for j in range(8):
                    p.chain(v, "dve", v.tensor_reduce(r_mj[:], sc(r_b, [[1024, 128], [64, 16], [1, 64]]), axis=mybir.AxisListType.X, op=ALU.max))
                    p.chain(v, "dve", v.tensor_tensor(sc(r_sfc, [[1024, 128], [64, 16], [1, 64]]), sc(r_b, [[1024, 128], [64, 16], [1, 64]]), sc(r_mj, [[16, 128], [1, 16], [0, 64]]), ALU.is_equal))
                    p.chain(v, "dve", v.tensor_tensor(r_a[:], r_a[:], r_sfc[:], ALU.add))
                    p.chain(v, "dve", v.tensor_tensor(r_c[:], r_sfc[:], r_scores[:], ALU.mult))
                    p.chain(v, "dve", v.tensor_reduce(r_mj[:], sc(r_c, [[1024, 128], [64, 16], [1, 64]]), axis=mybir.AxisListType.X, op=ALU.max))
                    p.chain(v, "dve", v.tensor_tensor(r_dn[:], r_dn[:], r_mj[:], ALU.add))
                    p.chain(v, "dve", v.scalar_tensor_tensor(r_b[:], r_sfc[:], -BIG, r_b[:], ALU.mult, ALU.add))
                p.chain(v, "dve", v.tensor_tensor(r_a[:], r_a[:], r_scores[:], ALU.mult))
                i = v.tensor_scalar(r_dn[:], r_dn[:], 1.0e-20, 0.0, ALU.add, ALU.add)
                p.inc(i, "dve", label=f"{tag}_dn")
                p.wait(v, "act", label=f"{tag}_rcp")
                p.chain(v, "dve", v.tensor_tensor(sc(r_sfc, [[1024, 128], [64, 16], [1, 64]]), sc(r_a, [[1024, 128], [64, 16], [1, 64]]), sc(r_mj, [[16, 128], [1, 16], [0, 64]]), ALU.mult))
                i = v.tensor_scalar(r_sfc[:], r_sfc[:], RSCALE, 0.0, ALU.mult, ALU.add)
                p.inc(i, "dve", label=f"{tag}_rw")
                for tc in range(TC):
                    p.wait(v, "pe", label=f"{tag}_tr{tc}")
                    i = v.tensor_copy(rwT_sb[0:64, tc * 128:(tc + 1) * 128],
                                      ps[6 + tc % 2][0:64, 0:128])
                    p.inc(i, "dve", label=f"{tag}_tev{tc}")

            def sy_fn(sy):
                if call == 0:
                    p.wait(sy, "pool", label="cast_pro")
                    for kc in range(HC):
                        i = sy.dma_start_transpose(gate_wT[:, kc * 64:(kc + 1) * 64],
                                                   _ap(gwbf, kc * 128, [[D, E], [1, 128]]))
                        p.dinc(i, "wload")
                    p.mark(f"{tag}_gwT", "wload")
                p.wait(sy, "dve", label=f"{tag}_tev{TC - 1}")
                i = sy.dma_start(rwT_d.ap(), rwT_sb[:])
                p.dinc(i, "store", label=f"{tag}_rwT")
                p.dwait(sy, "store", val=p.cnt["store"])

            p.on(blk, "tensor", t_fn)
            p.on(blk, "scalar", s_fn)
            p.on(blk, "vector", v_fn)
            p.on(blk, "sync", sy_fn)

    def ffn_units(call):
        tag = f"f{call}"
        nu = min(N_UNITS, len(UNITS))
        order = [("gu", 0), ("gu", 1), ("d", 0), ("gu", 2), ("d", 1), ("gu", 3), ("d", 2), ("d", 3)]

        with p.block() as blk:
            def sy_fn(sy):
                for ui in range(nu):
                    u = UNITS[ui]
                    iu, mc = u["i"], u["mc"]
                    if ui <= 2:
                        p.wait(sy, "pool", label="cast_pro")
                    else:
                        p.wait(sy, "pool", label=f"castm{ui - 3}")
                    if ui > 0:
                        p.wait(sy, "pe", label=f"{tag}_gdone{ui - 1}")
                    elif call == 1:
                        p.wait(sy, "pe", label=f"f0_gdone{nu - 1}")
                    for hc in range(HC):
                        i = sy.dma_start_transpose(wgT_k[hc][:],
                                                   _ap(gbf[ui], hc * 768 * 128, [[128, 768], [1, 128]]))
                        p.dinc(i, "wg")
                    p.mark(f"{tag}_gld{ui}", "wg")
                    if ui > 0:
                        p.wait(sy, "pe", label=f"{tag}_udone{ui - 1}")
                    elif call == 1:
                        p.wait(sy, "pe", label=f"f0_udone{nu - 1}")
                    for hc in range(HC):
                        i = sy.dma_start_transpose(wuT_k[hc][:],
                                                   _ap(ubf[ui], hc * 768 * 128, [[128, 768], [1, 128]]))
                        p.dinc(i, "wu")
                    p.mark(f"{tag}_uld{ui}", "wu")
                    if ui > 0:
                        p.wait(sy, "pe", label=f"{tag}_ddone{ui - 1}")
                    elif call == 1:
                        p.wait(sy, "pe", label=f"f0_ddone{nu - 1}")
                    for ic in range(mc):
                        i = sy.dma_start_transpose(wdT_i[ic][:],
                                                   _ap(dbf[ui], ic * D * 128, [[128, D], [1, 128]]))
                        p.dinc(i, "wd")
                    p.mark(f"{tag}_dld{ui}", "wd")
                    if u["kind"] == "e":
                        p.dwait(sy, "store", label=f"rt{call}_rwT")
                        lp = None
                        for uj in range(ui - 1, -1, -1):
                            if UNITS[uj]["kind"] == "e":
                                lp = uj
                                break
                        if lp is not None:
                            p.wait(sy, "pool", label=f"{tag}_pl{lp}")
                        elif call == 1:
                            p.wait(sy, "pool", label=f"f0_pl{63 if nu > 63 else nu - 1}")
                        i = sy.dma_start(rw_bc[:], _ap(rwT_d, u["e"] * T, [[0, 128], [1, T]]))
                        p.dinc(i, "wrw")
                        p.mark(f"{tag}_rwld{ui}", "wrw")
                if DEV_UPART == "load":
                    p.dwait(sy, "wload")

            def t_fn(t):
                if DEV_UPART == "load":
                    return
                for ui in range(nu):
                    u = UNITS[ui]
                    mc, lc = u["mc"], u["lc"]
                    for kind, nt in order:
                        if kind == "d" and DEV_UPART == "gu":
                            continue
                        if kind == "gu":
                            for m in range(mc):
                                mw = lc if m == mc - 1 else 128
                                j = GCH[(call, ui, nt, m)]
                                if nt == 0 and m == 0:
                                    p.dwait(t, "wg", label=f"{tag}_gld{ui}")
                                if j >= 2:
                                    p.wait(t, "act", label=f"{tag}_silu{j - 2}")
                                for kc in range(HC):
                                    i = t.matmul(ps[j % 2][0:mw, :],
                                                 wgT_k[kc][:, m * 128: m * 128 + mw],
                                                 xn[kc][:, nt * 512:(nt + 1) * 512],
                                                 start=(kc == 0), stop=(kc == HC - 1))
                                p.inc(i, "pe", label=f"{tag}_g{j}")
                                if nt == 0 and m == 0:
                                    p.dwait(t, "wu", label=f"{tag}_uld{ui}")
                                if j >= 2:
                                    p.wait(t, "dve", label=f"{tag}_mult{j - 2}")
                                for kc in range(HC):
                                    i = t.matmul(ps[2 + j % 2][0:mw, :],
                                                 wuT_k[kc][:, m * 128: m * 128 + mw],
                                                 xn[kc][:, nt * 512:(nt + 1) * 512],
                                                 start=(kc == 0), stop=(kc == HC - 1))
                                p.inc(i, "pe", label=f"{tag}_u{j}")
                            if nt == 3:
                                jl = GCH[(call, ui, 3, mc - 1)]
                                if p.dry:
                                    p.marks[f"{tag}_gdone{ui}"] = p.marks[f"{tag}_g{jl}"]
                                    p.marks[f"{tag}_udone{ui}"] = p.marks[f"{tag}_u{jl}"]
                        else:
                            if nt == 0:
                                p.dwait(t, "wd", label=f"{tag}_dld{ui}")
                            if u["kind"] == "e":
                                p.wait(t, "pool", label=f"{tag}_hact{ui}_{nt}")
                            else:
                                p.wait(t, "dve", label=f"{tag}_hact{ui}_{nt}")
                            for hc in range(HC):
                                dj = DCH[(call, ui, nt, hc)]
                                if dj >= 2:
                                    p.wait(t, "dve", label=f"{tag}_dev{dj - 2}")
                                for ic in range(mc):
                                    kw = lc if ic == mc - 1 else 128
                                    i = t.matmul(ps[4 + dj % 2][:, :],
                                                 wdT_i[ic][0:kw, hc * 128: hc * 128 + 128],
                                                 hact[nt % 2][0:kw, ic * 512:(ic + 1) * 512],
                                                 start=(ic == 0), stop=(ic == mc - 1))
                                p.inc(i, "pe", label=f"{tag}_d{dj}")
                            p.mark(f"{tag}_dcol{ui}_{nt}", "pe")
                            if nt == 3:
                                p.mark(f"{tag}_ddone{ui}", "pe")

            def s_fn(s):
                if DEV_UPART == "load":
                    return
                for ui in range(nu):
                    u = UNITS[ui]
                    mc, lc = u["mc"], u["lc"]
                    for nt in range(NT):
                        for m in range(mc):
                            mw = lc if m == mc - 1 else 128
                            j = GCH[(call, ui, nt, m)]
                            p.wait(s, "pe", label=f"{tag}_g{j}")
                            if j >= 2:
                                p.wait(s, "dve", label=f"{tag}_mult{j - 2}")
                            i = s.activation(sg_st[j % 2][0:mw, :], ps[j % 2][0:mw, :], AF.Silu)
                            p.inc(i, "act", label=f"{tag}_silu{j}")

            def v_fn(v):
                if DEV_UPART == "load":
                    return
                for ui in range(nu):
                    u = UNITS[ui]
                    mc, lc = u["mc"], u["lc"]
                    routed = u["kind"] == "e"
                    for kind, nt in order:
                        if kind == "d" and DEV_UPART == "gu":
                            continue
                        if kind == "gu":
                            for m in range(mc):
                                mw = lc if m == mc - 1 else 128
                                j = GCH[(call, ui, nt, m)]
                                p.wait(v, "pe", label=f"{tag}_u{j}")
                                p.wait(v, "act", label=f"{tag}_silu{j}")
                                if routed:
                                    if j >= 2:
                                        p.wait(v, "pool", label=f"{tag}_rwm{j - 2}")
                                    i = v.tensor_tensor(sgu_st[j % 2][0:mw, :], sg_st[j % 2][0:mw, :],
                                                        ps[2 + j % 2][0:mw, :], ALU.mult)
                                    p.inc(i, "dve", label=f"{tag}_mult{j}")
                                else:
                                    gidx = ui * 4 + nt
                                    if gidx >= 2:
                                        ui2, nt2 = (gidx - 2) // 4, (gidx - 2) % 4
                                        p.wait(v, "pe", label=f"{tag}_dcol{ui2}_{nt2}")
                                    i = v.tensor_tensor(hact[nt % 2][0:mw, m * 512:(m + 1) * 512],
                                                        sg_st[j % 2][0:mw, :],
                                                        ps[2 + j % 2][0:mw, :], ALU.mult)
                                    c = p.inc(i, "dve", label=f"{tag}_mult{j}")
                                    if p.dry and m == mc - 1:
                                        p.marks[f"{tag}_hact{ui}_{nt}"] = c
                        else:
                            for hc in range(HC):
                                dj = DCH[(call, ui, nt, hc)]
                                p.wait(v, "pe", label=f"{tag}_d{dj}")
                                i = v.scalar_tensor_tensor(x_fm[hc][:, nt * 512:(nt + 1) * 512],
                                                           ps[4 + dj % 2][:, :], 0.5,
                                                           x_fm[hc][:, nt * 512:(nt + 1) * 512],
                                                           ALU.mult, ALU.add)
                                p.inc(i, "dve", label=f"{tag}_dev{dj}")

            def g_fn(g):
                if DEV_UPART == "load":
                    return
                for ui in range(nu):
                    u = UNITS[ui]
                    casted = False
                    if call == 0 and ui + 3 < nu:
                        emit_unit_casts(g, ui + 3)
                        casted = True
                    if u["kind"] != "e":
                        if casted:
                            p.dwait(g, "cast", val=p.cnt["cast"])
                            i = g.memset(pdum[:], 0.0)
                            p.inc(i, "pool", label=f"castm{ui}")
                        continue
                    mc, lc = u["mc"], u["lc"]
                    p.dwait(g, "wrw", label=f"{tag}_rwld{ui}")
                    for nt in range(NT):
                        gidx = ui * 4 + nt
                        if gidx >= 2:
                            ui2, nt2 = (gidx - 2) // 4, (gidx - 2) % 4
                            p.wait(g, "pe", label=f"{tag}_dcol{ui2}_{nt2}")
                        for m in range(mc):
                            mw = lc if m == mc - 1 else 128
                            j = GCH[(call, ui, nt, m)]
                            p.wait(g, "dve", label=f"{tag}_mult{j}")
                            i = g.tensor_tensor(hact[nt % 2][0:mw, m * 512:(m + 1) * 512],
                                                sgu_st[j % 2][0:mw, :],
                                                rw_bc[0:mw, nt * 512:(nt + 1) * 512], ALU.mult)
                            c = p.inc(i, "pool", label=f"{tag}_rwm{j}")
                            if p.dry and m == mc - 1:
                                p.marks[f"{tag}_hact{ui}_{nt}"] = c
                    if casted:
                        p.dwait(g, "cast", val=p.cnt["cast"])
                        i = g.memset(pdum[:], 0.0)
                        p.inc(i, "pool", label=f"castm{ui}")
                    p.mark(f"{tag}_pl{ui}", "pool")

            p.on(blk, "sync", sy_fn)
            p.on(blk, "tensor", t_fn)
            p.on(blk, "scalar", s_fn)
            p.on(blk, "vector", v_fn)
            p.on(blk, "gpsimd", g_fn)

    def attn_consts():
        with p.block() as blk:
            def sy_fn(sy):
                p.dinc(sy.dma_start(cos_sb[:], cos_c), "wload")
                p.dinc(sy.dma_start(sin_sb[:], sin_c), "wload")
                p.dinc(sy.dma_start(mask_sb[:], mask_c), "wload")
                p.mark("at_consts", "wload")
            p.on(blk, "sync", sy_fn)

            def g_fn(g):
                p.dinc(g.dma_start(vb_row[:], _ap(vb_i.tensor, 0, [[D, 1], [1, D]])), "cast",
                       label="at_vb")
                i = g.memset(v_tm[:], 1.0)
                p.inc(i, "pool", label="at_vones")
            p.on(blk, "gpsimd", g_fn)

    def attn_pass(hg):
        tag = "at"
        with p.block() as blk:
            def g_fn(g):
                for wi, wt in enumerate([wq, wk, wv, wo]):
                    if wi == 0:
                        if hg == 1:
                            p.wait(g, "pe", label=f"{tag}_wfree1_0")
                    else:
                        p.wait(g, "pe", label=f"{tag}_wfree{hg}_{wi}")
                    if wi < 3:
                        src = _ap(wt.tensor, 512 * hg, [[D, 128], [128 * D, HC], [1, 512]])
                        dst = _ap(whalf, 0, [[4096, 128], [512, HC], [1, 512]])
                    else:
                        src = _ap(wt.tensor, 512 * hg * D, [[D, 128], [128 * D, 4], [1, 1024]])
                        dst = _ap(whalf, 0, [[4096, 128], [1024, 4], [1, 1024]])
                    p.dinc(g.dma_start(dst, src), "cast", label=f"{tag}_w{hg}_{wi}")

            def t_fn(t):
                cc = 0
                for wi in range(3):
                    p.dwait(t, "cast", label=f"{tag}_w{hg}_{wi}")
                    if wi < 2:
                        for ocl in range(4):
                            for tn in range(NT):
                                if cc >= 2:
                                    p.wait(t, "dve", label=f"{tag}_pj{hg}_{cc - 2}")
                                for kc in range(HC):
                                    i = t.matmul(ps[4 + cc % 2][:, :],
                                                 _ap(whalf, kc * 512 + ocl * 128, [[4096, 128], [1, 128]]),
                                                 xn[kc][:, tn * 512:(tn + 1) * 512],
                                                 start=(kc == 0), stop=(kc == HC - 1))
                                p.inc(i, "pe", label=f"{tag}_pc{hg}_{cc}")
                                cc += 1
                    else:
                        p.wait(t, "pool", label="ones_ready")
                        p.dwait(t, "wload", label="at_consts")
                        for tc in range(TC):
                            if cc >= 34:
                                p.wait(t, "act", label=f"{tag}_vev{hg}_{cc - 2}")
                            elif cc >= 2:
                                p.wait(t, "dve", label=f"{tag}_pj{hg}_{cc - 2}")
                            for kc in range(HC):
                                t.matmul(ps[4 + cc % 2][:, :],
                                         xn[kc][:, tc * 128:(tc + 1) * 128],
                                         _ap(whalf, kc * 512, [[4096, 128], [1, 512]]),
                                         start=(kc == 0), stop=False)
                            i = t.matmul(ps[4 + cc % 2][:, :], ones_rowb[:],
                                         _ap(vb_row, 512 * hg, [[D, 1], [1, 512]]),
                                         start=False, stop=True)
                            p.inc(i, "pe", label=f"{tag}_pc{hg}_{cc}")
                            cc += 1
                    p.mark(f"{tag}_wfree{hg}_{wi + 1}", "pe")
                if DEV_APART == "qkv":
                    return
                rp = 0
                for qk, src in enumerate((qh, kh)):
                    for c in range(4):
                        for tn in range(NT):
                            p.wait(t, "dve", label=f"{tag}_pj{hg}_{(0 if qk == 0 else 16) + c * 4 + tn}")
                            if rp >= 2:
                                p.wait(t, "dve", label=f"{tag}_rt1_{hg}_{rp - 2}")
                            i = t.matmul(ps[6 + rp % 2][:, :], perm[:],
                                         src[c][:, tn * 512:(tn + 1) * 512], start=True, stop=True)
                            p.inc(i, "pe", label=f"{tag}_perm{hg}_{rp}")
                            rp += 1
                if DEV_APART == "rope":
                    return
                p.wait(t, "dve", label=f"{tag}_rt1_{hg}_{31}")
                p.wait(t, "dve", label=f"{tag}_rt1_{hg}_{30}")
                for c in range(4):
                    for tn in range(NT):
                        for qk in range(2):
                            p.wait(t, "act", label=f"{tag}_bsq{hg}_{c}_{tn}_{qk}")
                            i = t.matmul(ps[7][0:1, 0:512], ones_col[:], sqst[:],
                                         start=True, stop=True)
                            p.inc(i, "pe", label=f"{tag}_bmm{hg}_{c}_{tn}_{qk}")
                            p.wait(t, "dve", label=f"{tag}_bmx{hg}_{c}_{tn}_{qk}")
                p.wait(t, "dve", label=f"{tag}_nbv{hg}")
                for c in range(4):
                    i = t.matmul(ps[7][:, 0:1], ones_row[:], nmax[0:1, 8 + c:9 + c],
                                 start=True, stop=True)
                    p.inc(i, "pe", label=f"{tag}_nbb{hg}_{c}")
                    p.wait(t, "act", label=f"{tag}_nbc{hg}_{c}")
                if DEV_APART == "bound":
                    return
                sti = 0
                cgi = 0
                last_odd = [None]
                for hl in range(8):
                    c, rb = hl // 2, 64 * (hl % 2)
                    odd = hl % 2 == 1
                    for qb in range(NT):
                        kmax = 4 * qb + 4

                        def ctx_mm(kcp, last):
                            sip = sti + kcp
                            if kcp == 0 and cgi >= 1:
                                p.wait(t, "act", label=f"{tag}_cst{hg}_{cgi - 1}")
                            if odd and kcp == 0 and last_odd[0] is not None:
                                p.wait(t, "act", label=f"{tag}_rsx{hg}_{last_odd[0]}")
                            diag = kcp >= 4 * qb
                            p.wait(t, "dve" if diag else "act",
                                   label=(f"{tag}_msk{hg}_{sip}" if diag else f"{tag}_exp{hg}_{sip}"))
                            if not odd:
                                i = t.matmul(ps[2][0:65, :],
                                             _ap(v_tm, kcp * 520 + 65 * hl, [[520 * 16, 128], [1, 65]]),
                                             attnT[sip % 2][:], start=(kcp == 0), stop=last)
                                p.inc(i, "pe", label=f"{tag}_ctx{hg}_{sip}")
                            else:
                                i = t.matmul(ps[6][0:1, :], ones_colb[:, 0:1],
                                             attnT[sip % 2][:], start=(kcp == 0), stop=last)
                                if last:
                                    p.inc(i, "pe", label=f"{tag}_sum{hg}_{sip}")
                                i = t.matmul(ps[2][64:128, :],
                                             _ap(v_tm, kcp * 520 + 65 * hl, [[520 * 16, 128], [1, 64]]),
                                             attnT[sip % 2][:], start=(kcp == 0), stop=last,
                                             tile_position=(0, 64))
                                p.inc(i, "pe", label=f"{tag}_ctx{hg}_{sip}")

                        for kc in range(kmax):
                            si = sti + kc
                            if si >= 2:
                                p.wait(t, "act", label=f"{tag}_exp{hg}_{si - 2}")
                            i = t.matmul(ps[si % 2][:, :],
                                         kh[c][rb:rb + 64, kc * 128:(kc + 1) * 128],
                                         qh[c][rb:rb + 64, qb * 512:(qb + 1) * 512],
                                         start=True, stop=True)
                            p.inc(i, "pe", label=f"{tag}_sT{hg}_{si}")
                            if kc >= 1:
                                ctx_mm(kc - 1, kc - 1 == kmax - 1)
                        ctx_mm(kmax - 1, True)
                        sti += kmax
                        p.wait(t, "act", label=f"{tag}_rsx{hg}_{cgi}")
                        if cgi >= 1:
                            p.wait(t, "dve", label=f"{tag}_ctxev{hg}_{cgi - 1}")
                        i = t.matmul(ps[3][64:128, :] if odd else ps[3][0:64, :],
                                     ones_rowb[0:1, 0:64], rs_bf[0:1, 0:512], start=True, stop=True,
                                     tile_position=(0, 64) if odd else None)
                        p.inc(i, "pe", label=f"{tag}_rb{hg}_{cgi}")
                        if odd:
                            last_odd[0] = cgi
                        cgi += 1
                if DEV_APART in ("head", "h1", "h2", "h2a", "h2b"):
                    return
                p.dwait(t, "cast", label=f"{tag}_w{hg}_3")
                p.wait(t, "dve", label=f"{tag}_ctxev{hg}_{cgi - 1}")
                oj = 0
                for oc in range(8):
                    for tn in range(NT):
                        if oj >= 2:
                            p.wait(t, "dve", label=f"{tag}_oev{hg}_{oj - 2}")
                        for kcl in range(4):
                            i = t.matmul(ps[4 + oj % 2][:, :],
                                         _ap(whalf, kcl * 1024 + oc * 128, [[4096, 128], [1, 128]]),
                                         ctxh[kcl][:, tn * 512:(tn + 1) * 512],
                                         start=(kcl == 0), stop=(kcl == 3))
                        p.inc(i, "pe", label=f"{tag}_oc{hg}_{oj}")
                        oj += 1
                p.mark(f"{tag}_wfree{hg + 1}_0", "pe")

            def s_fn(s):
                for tc in range(TC):
                    cc = 32 + tc
                    p.wait(s, "pe", label=f"{tag}_pc{hg}_{cc}")
                    if tc >= 1:
                        p.wait(s, "dve", label=f"{tag}_pj{hg}_{cc - 1}")
                    i = s.copy(v_st[:], ps[4 + cc % 2][:, :])
                    p.inc(i, "act", label=f"{tag}_vev{hg}_{cc}")
                if DEV_APART == "qkv":
                    return
                if DEV_APART == "rope":
                    return
                first_sq = [True]
                for c in range(4):
                    for tn in range(NT):
                        for qk, src in enumerate((qh, kh)):
                            p.wait(s, "dve", label=f"{tag}_rfin{hg}_{qk}_{c}_{tn}")
                            p.wait(s, "pe", label=f"{tag}_bprev{hg}_{c}_{tn}_{qk}")
                            if first_sq[0]:
                                p.wait(s, "dve", label=f"{tag}_pj{hg}_47")
                                first_sq[0] = False
                            i = s.square(sqst[:], src[c][:, tn * 512:(tn + 1) * 512])
                            p.inc(i, "act", label=f"{tag}_bsq{hg}_{c}_{tn}_{qk}")
                p.wait(s, "dve", label=f"{tag}_nmaxv{hg}")
                i = s.activation(nmax[0:1, 0:8], nmax[0:1, 0:8], AF.Sqrt)
                p.inc(i, "act", label=f"{tag}_nsq{hg}")
                for c in range(4):
                    p.wait(s, "pe", label=f"{tag}_nbb{hg}_{c}")
                    i = s.copy(nb_sb[c][:], ps[7][:, 0:1])
                    p.inc(i, "act", label=f"{tag}_nbc{hg}_{c}")
                if DEV_APART == "bound":
                    return
                sti = 0
                cgi = 0
                for hl in range(8):
                    c = hl // 2
                    odd = hl % 2 == 1
                    for qb in range(NT):
                        kmax = 4 * qb + 4
                        for kc in range(kmax):
                            p.wait(s, "pe", label=f"{tag}_sT{hg}_{sti}")
                            if sti >= 2:
                                p.wait(s, "pe", label=f"{tag}_ctx{hg}_{sti - 2}")
                            i = s.activation(attnT[sti % 2][:], ps[sti % 2][:], AF.Exp,
                                             bias=nb_sb[c][:, 0:1], scale=0.125)
                            p.inc(i, "act", label=f"{tag}_exp{hg}_{sti}")
                            sti += 1
                        p.wait(s, "pe", label=f"{tag}_ctx{hg}_{sti - 1}")
                        if not odd:
                            c_ = p.inc(s.copy(ctx_st[0:65, :], ps[2][0:65, :]),
                                       "act", label=f"{tag}_cst{hg}_{cgi}")
                            if not p.dry:
                                s.wait_ge(p.sem["act"], c_)
                            p.chain(s, "act", s.activation(rs_sb[0:1, :], ctx_st[64:65, :], AF.Ln))
                        else:
                            p.wait(s, "pe", label=f"{tag}_sum{hg}_{sti - 1}")
                            c_ = p.inc(s.copy(ctx_st[64:128, :], ps[2][64:128, :]),
                                       "act", label=f"{tag}_cst{hg}_{cgi}")
                            if not p.dry:
                                s.wait_ge(p.sem["act"], c_)
                            p.chain(s, "act", s.activation(rs_sb[0:1, :], ps[6][0:1, :], AF.Ln))
                        i = s.activation(rs_bf[0:1, 0:512], rs_sb[0:1, :], AF.Exp, scale=-1.0)
                        p.inc(i, "act", label=f"{tag}_rsx{hg}_{cgi}")
                        cgi += 1

            def v_fn(v):
                cc = 0
                for wi in range(3):
                    if wi < 2:
                        dstl = qh if wi == 0 else kh
                        bias = qb_pp if wi == 0 else kb_pp
                        for ocl in range(4):
                            for tn in range(NT):
                                p.wait(v, "pe", label=f"{tag}_pc{hg}_{cc}")
                                i = v.tensor_scalar(dstl[ocl][:, tn * 512:(tn + 1) * 512],
                                                    ps[4 + cc % 2][:, :],
                                                    bias[:, 4 * hg + ocl:4 * hg + ocl + 1], 0.0,
                                                    ALU.add, ALU.add)
                                p.inc(i, "dve", label=f"{tag}_pj{hg}_{cc}")
                                cc += 1
                    else:
                        p.wait(v, "pool", label="at_vones")
                        for tc in range(TC):
                            p.wait(v, "act", label=f"{tag}_vev{hg}_{cc}")
                            i = v.tensor_copy(_ap(v_tm, tc * 520, [[520 * 16, 128], [65, 8], [1, 64]]),
                                              v_st[:])
                            p.inc(i, "dve", label=f"{tag}_pj{hg}_{cc}")
                            cc += 1
                if DEV_APART == "qkv":
                    return
                rp = 0
                for qk, src in enumerate((qh, kh)):
                    for c in range(4):
                        for tn in range(NT):
                            p.chain(v, "dve", v.tensor_tensor(rope_t0[:],
                                              src[c][:, tn * 512:(tn + 1) * 512],
                                              cos_sb[:, tn * 512:(tn + 1) * 512], ALU.mult))
                            p.wait(v, "pe", label=f"{tag}_perm{hg}_{rp}")
                            i = v.tensor_tensor(rope_t1[:], ps[6 + rp % 2][:, :],
                                                sin_sb[:, tn * 512:(tn + 1) * 512], ALU.mult)
                            c_ = p.inc(i, "dve", label=f"{tag}_rt1_{hg}_{rp}")
                            if not p.dry:
                                v.wait_ge(p.sem["dve"], c_)
                            i = v.tensor_tensor(src[c][:, tn * 512:(tn + 1) * 512],
                                                rope_t0[:], rope_t1[:], ALU.add)
                            p.inc(i, "dve", label=f"{tag}_rfin{hg}_{qk}_{c}_{tn}")
                            p.wait(v, "dve", val=p.cnt["dve"])
                            rp += 1
                if DEV_APART == "rope":
                    return
                for c in range(4):
                    for tn in range(NT):
                        for qk in range(2):
                            p.wait(v, "pe", label=f"{tag}_bmm{hg}_{c}_{tn}_{qk}")
                            col = qk * 4 + c
                            if tn == 0:
                                i = v.tensor_reduce(nmax[0:1, col:col + 1], ps[7][0:1, 0:512],
                                                    axis=mybir.AxisListType.X, op=ALU.max)
                            else:
                                p.chain(v, "dve", v.tensor_reduce(nmax[0:1, 15:16], ps[7][0:1, 0:512],
                                                axis=mybir.AxisListType.X, op=ALU.max))
                                i = v.tensor_tensor(nmax[0:1, col:col + 1], nmax[0:1, col:col + 1],
                                                    nmax[0:1, 15:16], ALU.max)
                            p.inc(i, "dve", label=f"{tag}_bmx{hg}_{c}_{tn}_{qk}")
                p.mark(f"{tag}_nmaxv{hg}", "dve")
                p.wait(v, "act", label=f"{tag}_nsq{hg}")
                p.chain(v, "dve", v.tensor_tensor(nmax[0:1, 8:12], nmax[0:1, 0:4], nmax[0:1, 4:8], ALU.mult))
                i = v.tensor_scalar(nmax[0:1, 8:12], nmax[0:1, 8:12], -0.125, 0.0, ALU.mult, ALU.add)
                p.inc(i, "dve", label=f"{tag}_nbv{hg}")
                if DEV_APART == "bound":
                    return
                sti = 0
                cgi = 0
                for hl in range(8):
                    c, rb = hl // 2, 64 * (hl % 2)
                    for qb in range(NT):
                        kmax = 4 * qb + 4
                        for kc in range(kmax):
                            if kc >= 4 * qb:
                                rel = kc - 4 * qb
                                p.wait(v, "act", label=f"{tag}_exp{hg}_{sti}")
                                i = v.tensor_tensor(attnT[sti % 2][:], attnT[sti % 2][:],
                                                    mask_sb[:, rel * 512:(rel + 1) * 512], ALU.mult)
                                p.inc(i, "dve", label=f"{tag}_msk{hg}_{sti}")
                            sti += 1
                        p.wait(v, "act", label=f"{tag}_cst{hg}_{cgi}")
                        p.wait(v, "pe", label=f"{tag}_rb{hg}_{cgi}")
                        if hl % 2 == 0:
                            i = v.tensor_tensor(ctxh[c][0:64, qb * 512:(qb + 1) * 512],
                                                ctx_st[0:64, :], ps[3][0:64, :], ALU.mult)
                        else:
                            i = v.tensor_tensor(ctxh[c][64:128, qb * 512:(qb + 1) * 512],
                                                ctx_st[64:128, :], ps[3][64:128, :], ALU.mult)
                        p.inc(i, "dve", label=f"{tag}_ctxev{hg}_{cgi}")
                        cgi += 1
                if DEV_APART in ("head", "h1", "h2", "h2a", "h2b"):
                    return
                for oj in range(32):
                    oc, tn = oj // 4, oj % 4
                    p.wait(v, "pe", label=f"{tag}_oc{hg}_{oj}")
                    if hg == 0:
                        i = v.scalar_tensor_tensor(x_fm[oc][:, tn * 512:(tn + 1) * 512],
                                                   ps[4 + oj % 2][:, :], ob_pp[:, oc:oc + 1],
                                                   x_fm[oc][:, tn * 512:(tn + 1) * 512],
                                                   ALU.add, ALU.add)
                    else:
                        i = v.tensor_tensor(x_fm[oc][:, tn * 512:(tn + 1) * 512],
                                            x_fm[oc][:, tn * 512:(tn + 1) * 512],
                                            ps[4 + oj % 2][:, :], ALU.add)
                    p.inc(i, "dve", label=f"{tag}_oev{hg}_{oj}")

            p.on(blk, "gpsimd", g_fn)
            p.on(blk, "tensor", t_fn)
            p.on(blk, "scalar", s_fn)
            p.on(blk, "vector", v_fn)

    def register_bprev():
        for hg in range(2):
            prev = None
            for c in range(4):
                for tn in range(NT):
                    for qk in range(2):
                        lab = f"at_bprev{hg}_{c}_{tn}_{qk}"
                        if prev is None:
                            p.marks[lab] = p.marks.get(f"at_bmm{1 - hg}_3_3_1", 0) if hg == 1 else 0
                        else:
                            p.marks[lab] = p.marks.get(f"at_bmm{hg}_{prev[0]}_{prev[1]}_{prev[2]}", 0)
                        prev = (c, tn, qk)

    def output_stage():
        with p.block() as blk:
            def sy_fn(sy):
                p.dwait(sy, "wload", label="prologue_loads")
                for c in range(HC):
                    i = sy.dma_start(y_out[c * 128:(c + 1) * 128, :], x_fm[c][:])
                    p.dinc(i, "store")
                if DEV_STAGE:
                    for c in range(HC):
                        p.dinc(sy.dma_start(xn_out[c * 128:(c + 1) * 128, :], xn[c][:]), "store")
                    if DEV_STAGE == "rt":
                        p.dinc(sy.dma_start(rw_out.tensor.ap(), rwT_sb[:]), "store")
                        p.dinc(sy.dma_start(dn_out, r_dn[:]), "store")
                        p.dinc(sy.dma_start(rcp_out, r_mj[:]), "store")
                        p.dinc(sy.dma_start(sco_out, r_scores[:]), "store")
                        p.dinc(sy.dma_start(sel_out, r_a[:]), "store")
                p.dwait(sy, "store")
            p.on(blk, "sync", sy_fn)

    def emit_all():
        prologue()
        if DO_FFN1:
            ln_stage(0)
            if DEV_STAGE != "ln":
                router_stage(0)
                if DEV_STAGE != "rt":
                    ffn_units(0)
        if DO_ATTN:
            ln_stage(2)
            attn_consts()
            attn_pass(0)
            attn_pass(1)
        if DO_FFN2:
            ln_stage(1)
            router_stage(1)
            ffn_units(1)
        output_stage()

    p.start(dry=True)
    emit_all()
    register_bprev()
    final_cnt = dict(p.cnt)
    p.start(dry=False)
    emit_all()
    assert final_cnt == p.cnt, (final_cnt, p.cnt)
    return nc



def _make_consts():
    import ml_dtypes
    j = np.arange(128) % 32
    ts = float(10000.0) ** (2.0 * j / HD)
    pos = np.arange(T)
    ang = pos[None, :] / ts[:, None]
    cos_t = np.cos(ang).astype(ml_dtypes.bfloat16)
    sin_t = np.sin(ang).astype(ml_dtypes.bfloat16)
    perm = np.zeros((128, 128), np.float32)
    for r in range(128):
        b = (r // 64) * 64
        jj = r - b
        if jj < 32:
            perm[b + jj + 32, r] = -1.0
        else:
            perm[b + jj - 32, r] = 1.0
    perm = perm.astype(ml_dtypes.bfloat16)
    masks = np.zeros((128, 4 * 512), np.float32)
    for ri, rel in enumerate((0, 128, 256, 384)):
        k = np.arange(128)[:, None] + rel
        q = np.arange(512)[None, :]
        masks[:, ri * 512:(ri + 1) * 512] = (k <= q).astype(np.float32)
    masks = masks.astype(ml_dtypes.bfloat16)
    iden = np.eye(128, dtype=np.float32)
    return cos_t, sin_t, perm, masks, iden


def _pad_gu(W, S):
    W = np.asarray(W, np.float32)
    S = np.asarray(S, np.float32)
    out = np.zeros((66, 768, D), np.float32)
    out[:64, :I_EXP, :] = W
    out[64, :768, :] = S[0:768]
    out[65, :640, :] = S[768:1408]
    return out


def _pad_d(W, S):
    W = np.asarray(W, np.float32)
    S = np.asarray(S, np.float32)
    out = np.zeros((66, D, 768), np.float32)
    out[:64, :, :I_EXP] = W
    out[64, :, :768] = S[:, 0:768]
    out[65, :, :640] = S[:, 768:1408]
    return out


def kernel(L0_attn_x, L0_attn_wq_w, L0_attn_wq_b, L0_attn_wk_w, L0_attn_wk_b,
           L0_attn_wv_w, L0_attn_wv_b, L0_attn_wo_w, L0_attn_wo_b,
           L0_ffn_gate_w, L0_ffn_gate_bias, L0_ffn_Wg, L0_ffn_Wu, L0_ffn_Wd,
           L0_ffn_Sg, L0_ffn_Su, L0_ffn_Sd):
    cos_t, sin_t, perm, masks, iden = _make_consts()
    nc = build_nc()
    common = dict(
        wq=np.ascontiguousarray(np.asarray(L0_attn_wq_w, np.float32)),
        wk=np.ascontiguousarray(np.asarray(L0_attn_wk_w, np.float32)),
        wv=np.ascontiguousarray(np.asarray(L0_attn_wv_w, np.float32)),
        wo=np.ascontiguousarray(np.asarray(L0_attn_wo_w, np.float32)),
        qb=np.asarray(L0_attn_wq_b, np.float32), kb=np.asarray(L0_attn_wk_b, np.float32),
        vb=np.asarray(L0_attn_wv_b, np.float32), ob=np.asarray(L0_attn_wo_b, np.float32),
        gate_w=np.ascontiguousarray(np.asarray(L0_ffn_gate_w, np.float32)),
        gate_b=np.asarray(L0_ffn_gate_bias, np.float32),
        Wg=_pad_gu(L0_ffn_Wg, L0_ffn_Sg), Wu=_pad_gu(L0_ffn_Wu, L0_ffn_Su),
        Wd=_pad_d(L0_ffn_Wd, L0_ffn_Sd),
        cos_t=cos_t, sin_t=sin_t, perm=perm, masks=masks, iden=iden,
    )
    x = np.asarray(L0_attn_x, np.float32)
    in_maps = []
    for c in range(8):
        m = dict(common)
        m["x_fm"] = np.ascontiguousarray(x[c].T)
        in_maps.append(m)
    res = run_bass_kernel_spmd(nc, in_maps, core_ids=list(range(8)))
    out = np.stack([np.asarray(r["y_fm"]).T for r in res.results], axis=0)
    return np.ascontiguousarray(out.astype(np.float32))


# version 02
# speedup vs baseline: 1.1302x; 1.1302x over previous
import numpy as np
import concourse.bass as bass
import concourse.mybir as mybir
from concourse.bass_utils import run_bass_kernel_spmd

F32 = mybir.dt.float32
BF16 = mybir.dt.bfloat16
AF = mybir.ActivationFunctionType
ALU = mybir.AluOpType

D = 1024
T = 2048
HC = 8
NT = 4
TC = 16
NH = 16
HD = 64
E = 64
I_EXP = 704
BIG = 1.0e30
LN_EPS = 1.0e-5
RSCALE = 2.5

UNITS = [dict(kind="e", e=e, i=I_EXP) for e in range(E)] + [
    dict(kind="sa", i=768), dict(kind="sb", i=640)]
for u in UNITS:
    u["mc"] = (u["i"] + 127) // 128
    u["lc"] = u["i"] - 128 * (u["mc"] - 1)

N_UNITS = 66
DO_FFN1 = True
DO_ATTN = True
DO_FFN2 = True
DEV_STAGE = ""
DEV_UPART = ""
DEV_APART = ""


def _ap(t, off, dims):
    return bass.AP(t, off, dims)


class _MockInstr:
    def then_inc(self, *a, **k):
        return self


class _MockEng:
    def wait_ge(self, *a, **k):
        return None

    def __getattr__(self, name):
        return lambda *a, **k: _MockInstr()


class _MockBlock:
    def __enter__(self):
        return self

    def __exit__(self, *a):
        return False


class P:

    def __init__(self, nc):
        self.nc = nc
        self.sem = {}
        for s in ["cast", "wload", "wg", "wu", "wd", "wrw", "store", "pe", "act", "dve", "pool"]:
            self.sem[s] = nc.semaphore(s).__enter__()
        self.marks = {}
        self.dry = True
        self.cnt = {}

    def start(self, dry):
        self.dry = dry
        self.cnt = {k: 0 for k in self.sem}

    def block(self):
        return _MockBlock() if self.dry else self.nc.Block()

    def on(self, blk, eng, fn):
        if self.dry:
            fn(_MockEng())
        else:
            getattr(blk, eng)(fn)

    def mark(self, label, s):
        if self.dry:
            self.marks[label] = self.cnt[s]

    def inc(self, instr, s, label=None, n=1):
        self.cnt[s] += n
        if not self.dry:
            instr.then_inc(self.sem[s], n)
        if label is not None:
            if self.dry:
                self.marks[label] = self.cnt[s]
            else:
                assert self.marks[label] == self.cnt[s], (label, self.marks[label], self.cnt[s])
        return self.cnt[s]

    def dinc(self, instr, s, label=None):
        self.cnt[s] += 1
        if not self.dry:
            instr.then_inc(self.sem[s], 16)
        if label is not None:
            if self.dry:
                self.marks[label] = self.cnt[s]
            else:
                assert self.marks[label] == self.cnt[s], label
        return self.cnt[s]

    STRICT = False

    def chain(self, eng, s, instr):
        c = self.inc(instr, s)
        if not self.dry:
            eng.wait_ge(self.sem[s], c)
        return c

    def wait(self, eng, s, label=None, val=None):
        if self.dry:
            return
        if val is None:
            if label is not None and label not in self.marks and P.STRICT:
                raise AssertionError(f"missing mark: {label}")
            val = self.marks.get(label, 0) if label is not None else self.cnt[s]
        if val <= 0:
            return
        eng.wait_ge(self.sem[s], val)

    def dwait(self, eng, s, label=None, val=None):
        if self.dry:
            return
        if val is None:
            if label is not None and label not in self.marks and P.STRICT:
                raise AssertionError(f"missing mark: {label}")
            val = self.marks.get(label, 0) if label is not None else self.cnt[s]
        if val <= 0:
            return
        eng.wait_ge(self.sem[s], 16 * val)


GCH = {}
DCH = {}
for _call in (0, 1):
    _gj = 0
    _dj = 0
    for _ui in range(len(UNITS)):
        _mc = UNITS[_ui]["mc"]
        for _nt in range(NT):
            for _m in range(_mc):
                GCH[(_call, _ui, _nt, _m)] = _gj
                _gj += 1
        for _nt in range(NT):
            for _hc in range(HC):
                DCH[(_call, _ui, _nt, _hc)] = _dj
                _dj += 1


def build_nc():
    nc = bass.Bass("TRN2", target_bir_lowering=False)

    def inpt(name, shape, dtype=F32):
        return nc.dram_tensor(name, shape, dtype, kind="ExternalInput").ap()

    x_in = inpt("x_fm", [D, T])
    wq = inpt("wq", [D, D]); wk = inpt("wk", [D, D])
    wv = inpt("wv", [D, D]); wo = inpt("wo", [D, D])
    qb_i = inpt("qb", [D]); kb_i = inpt("kb", [D]); vb_i = inpt("vb", [D]); ob_i = inpt("ob", [D])
    gate_w = inpt("gate_w", [E, D]); gate_b = inpt("gate_b", [E])
    Wg = inpt("Wg", [66, 768, D]); Wu = inpt("Wu", [66, 768, D]); Wd = inpt("Wd", [66, D, 768])
    cos_c = inpt("cos_t", [128, T], BF16)
    sin_c = inpt("sin_t", [128, T], BF16)
    perm_c = inpt("perm", [128, 128], BF16)
    mask_c = inpt("masks", [128, 4 * 512], BF16)
    iden_c = inpt("iden", [128, 128], F32)
    y_out = nc.dram_tensor("y_fm", [D, T], F32, kind="ExternalOutput").ap()
    if DEV_STAGE:
        xn_out = nc.dram_tensor("xn_out", [D, T], BF16, kind="ExternalOutput").ap()
        rw_out = nc.dram_tensor("rw_out", [E, T], F32, kind="ExternalOutput").ap()
        dn_out = nc.dram_tensor("dn_out", [128, 16], F32, kind="ExternalOutput").ap()
        rcp_out = nc.dram_tensor("rcp_out", [128, 16], F32, kind="ExternalOutput").ap()
        sco_out = nc.dram_tensor("sco_out", [128, 1024], F32, kind="ExternalOutput").ap()
        sel_out = nc.dram_tensor("sel_out", [128, 1024], F32, kind="ExternalOutput").ap()

    gbf, ubf, dbf = [], [], []
    for ui, u in enumerate(UNITS):
        gbf.append(nc.dram_tensor(f"gbf{ui}", [HC, 768, 128], BF16))
        ubf.append(nc.dram_tensor(f"ubf{ui}", [HC, 768, 128], BF16))
        dbf.append(nc.dram_tensor(f"dbf{ui}", [6, D, 128], BF16))
    gwbf = nc.dram_tensor("gwbf", [E, D], BF16)
    rwT_d = nc.dram_tensor("rwT_d", [E, T], F32)

    cur = [nc.sbuf_base]

    def alloc(name, shape, dtype, align=32):
        sz = shape[-1] * mybir.dt.size(dtype)
        cur[0] = (cur[0] + align - 1) // align * align
        t = nc.alloc_sbuf_tensor_at(name, shape, dtype, offset=cur[0])
        cur[0] += sz
        return t

    x_fm = [alloc(f"x{c}", [128, T], F32) for c in range(HC)]
    xn = [alloc(f"xn{c}", [128, T], BF16) for c in range(HC)]
    iden = alloc("iden_sb", [128, 128], F32)
    perm = alloc("perm_sb", [128, 128], BF16)
    gate_wT = alloc("gate_wT", [128, 8 * 64], BF16)
    gb_bc = alloc("gb_bc", [128, E], F32)
    qb_pp = alloc("qb_pp", [128, 8], F32)
    kb_pp = alloc("kb_pp", [128, 8], F32)
    ob_pp = alloc("ob_pp", [128, 8], F32)
    ones_col = alloc("ones_col", [128, 1], F32)
    ones_colb = alloc("ones_colb", [128, 1], BF16)
    pdum = alloc("pdum", [1, 1], F32)
    ones_row = alloc("ones_row", [1, 128], F32)
    ones_rowb = alloc("ones_rowb", [1, 128], BF16)
    persist_end = cur[0]

    cur[0] = persist_end
    wgT = alloc("wgT", [128, HC * 768], BF16)
    wuT = alloc("wuT", [128, HC * 768], BF16)
    wdT = alloc("wdT", [128, 6 * 1024], BF16)
    rw_bc = alloc("rw_bc", [128, T], F32)
    hact = [alloc(f"hact{c}", [128, 6 * 512], BF16) for c in range(2)]
    sg_st = [alloc(f"sg{i}", [128, 512], F32) for i in range(2)]
    sgu_st = [alloc(f"sgu{i}", [128, 512], F32) for i in range(2)]
    ffn_end = cur[0]

    cur[0] = persist_end
    qh = [alloc(f"qh{c}", [128, T], BF16) for c in range(4)]
    kh = [alloc(f"kh{c}", [128, T], BF16) for c in range(4)]
    v_tm = alloc("v_tm", [128, TC * 520], BF16)
    ctxh = [alloc(f"ctxh{c}", [128, T], BF16) for c in range(4)]
    whalf = alloc("whalf", [128, 4096], BF16)
    attnT = [alloc(f"attnT{i}", [128, 512], BF16) for i in range(2)]
    rope_t0 = attnT[0]
    rope_t1 = attnT[1]
    cos_sb = alloc("cos_sb", [128, T], BF16)
    sin_sb = alloc("sin_sb", [128, T], BF16)
    mask_sb = alloc("mask_sb", [128, 4 * 512], BF16)
    vb_row = alloc("vb_row", [1, D], BF16)
    v_st = alloc("v_st", [128, 512], F32)
    sqst = v_st
    nmax = alloc("nmax", [1, 16], F32)
    nb_sb = [alloc(f"nb{c}", [128, 1], F32) for c in range(4)]
    rs_sb = alloc("rs_sb", [1, 512], F32)
    rs_bf = vb_row
    ctx_st = v_st
    attn_end = cur[0]

    cur[0] = persist_end
    r_scores = alloc("r_scores", [128, 1024], F32)
    r_sfc = alloc("r_sfc", [128, 1024], F32)
    r_a = alloc("r_a", [128, 1024], F32)
    r_b = alloc("r_b", [128, 1024], F32)
    r_m1 = alloc("r_m1", [128, 128], F32)
    r_m2 = alloc("r_m2", [128, 128], F32)
    r_gs = alloc("r_gs", [128, 128], F32)
    r_gw = alloc("r_gw", [128, 128], F32)
    r_gm = alloc("r_gm", [128, 128], F32)
    r_mj = alloc("r_mj", [128, 16], F32)
    r_c = alloc("r_c", [128, 1024], F32)
    r_dn = alloc("r_dn", [128, 16], F32)
    rwT_sb = alloc("rwT_sb", [64, T], F32)
    ln_a = alloc("ln_a", [1, T], F32)
    ln_b = alloc("ln_b", [1, T], F32)
    ln_c = alloc("ln_c", [1, T], F32)
    ln_st = [alloc(f"lnst{i}", [128, 512], F32) for i in range(2)]
    xsq_st = [alloc(f"xsq{i}", [128, 512], F32) for i in range(8)]
    rout_end = cur[0]
    assert rout_end <= attn_end, (rout_end, attn_end)
    assert max(ffn_end, attn_end) <= nc.sbuf_top, (ffn_end, attn_end, nc.sbuf_top)

    ps = [nc.psum_tensor(f"ps{i}", [128, 512], F32).__enter__() for i in range(8)]

    p = P(nc)

    def emit_unit_casts(g, ui):
        sg_ = _ap(Wg.tensor, ui * 768 * D, [[128, HC], [D, 768], [1, 128]])
        su_ = _ap(Wu.tensor, ui * 768 * D, [[128, HC], [D, 768], [1, 128]])
        p.dinc(g.dma_start(_ap(gbf[ui], 0, [[768 * 128, HC], [128, 768], [1, 128]]), sg_), "cast")
        p.dinc(g.dma_start(_ap(ubf[ui], 0, [[768 * 128, HC], [128, 768], [1, 128]]), su_), "cast")
        sd_ = _ap(Wd.tensor, ui * D * 768, [[768, D], [128, 6], [1, 128]])
        p.dinc(g.dma_start(_ap(dbf[ui], 0, [[128, D], [D * 128, 6], [1, 128]]), sd_),
               "cast", label=f"cast_u{ui}")

    def prologue():
        with p.block() as blk:
            def sync_fn(sy):
                for c in range(HC):
                    p.dinc(sy.dma_start(x_fm[c][:], x_in[c * 128:(c + 1) * 128, :]), "wload")
                p.dinc(sy.dma_start(iden[:], iden_c), "wload")
                p.dinc(sy.dma_start(perm[:], perm_c), "wload")
                p.dinc(sy.dma_start(gb_bc[:], _ap(gate_b.tensor, 0, [[0, 128], [1, E]])), "wload")
                for oc in range(8):
                    p.dinc(sy.dma_start(qb_pp[:, oc:oc + 1], _ap(qb_i.tensor, oc * 128, [[1, 128], [1, 1]])), "wload")
                for oc in range(8):
                    p.dinc(sy.dma_start(kb_pp[:, oc:oc + 1], _ap(kb_i.tensor, oc * 128, [[1, 128], [1, 1]])), "wload")
                for oc in range(8):
                    p.dinc(sy.dma_start(ob_pp[:, oc:oc + 1], _ap(ob_i.tensor, oc * 128, [[1, 128], [1, 1]])), "wload")
                p.mark("prologue_loads", "wload")
            p.on(blk, "sync", sync_fn)

            def g_fn(g):
                g.memset(ones_col[:], 1.0)
                g.memset(ones_colb[:], 1.0)
                g.memset(ones_row[:], 1.0)
                i = g.memset(ones_rowb[:], 1.0)
                p.inc(i, "pool", label="ones_ready")
                if (DO_FFN1 or DO_FFN2) and DEV_STAGE != "ln":
                    p.dinc(g.dma_start(gwbf.ap(), gate_w), "cast", label="cast_gw")
                    if DEV_STAGE != "rt":
                        for ui in range(min(3, N_UNITS)):
                            emit_unit_casts(g, ui)
                    p.dwait(g, "cast", val=p.cnt["cast"])
                    i = g.memset(pdum[:], 0.0)
                    p.inc(i, "pool", label="cast_pro")
            p.on(blk, "gpsimd", g_fn)

    def ln_stage(call):
        tag = f"ln{call}"
        with p.block() as blk:
            def t_fn(t):
                p.dwait(t, "wload", label="prologue_loads")
                for nt in range(NT):
                    for hc in range(HC):
                        i = t.matmul(ps[6][0:1, :], ones_col[:], x_fm[hc][:, nt * 512:(nt + 1) * 512],
                                     start=(hc == 0), stop=(hc == HC - 1))
                    p.inc(i, "pe")
                    for hc in range(HC):
                        p.wait(t, "act", label=f"{tag}_sq{nt}_{hc}")
                        i = t.matmul(ps[7][0:1, :], ones_col[:], xsq_st[hc][:],
                                     start=(hc == 0), stop=(hc == HC - 1))
                    p.inc(i, "pe", label=f"{tag}_stats{nt}")
                    p.wait(t, "dve", label=f"{tag}_evac{nt}")
                p.wait(t, "dve", label=f"{tag}_rows")
                for nt in range(NT):
                    i = t.matmul(ps[6][:, :], ones_row[:], ln_c[0:1, nt * 512:(nt + 1) * 512],
                                 start=True, stop=True)
                    p.inc(i, "pe")
                    i = t.matmul(ps[7][:, :], ones_row[:], ln_a[0:1, nt * 512:(nt + 1) * 512],
                                 start=True, stop=True)
                    p.inc(i, "pe", label=f"{tag}_bc{nt}")
                    p.wait(t, "dve", label=f"{tag}_norm{nt}")

            def s_fn(s):
                p.dwait(s, "wload", label="prologue_loads")
                for nt in range(NT):
                    for hc in range(HC):
                        if nt > 0:
                            p.wait(s, "pe", label=f"{tag}_stats{nt - 1}")
                        i = s.square(xsq_st[hc][:], x_fm[hc][:, nt * 512:(nt + 1) * 512])
                        p.inc(i, "act", label=f"{tag}_sq{nt}_{hc}")
                p.wait(s, "dve", label=f"{tag}_var")
                p.chain(s, "act", s.activation(ln_c[:], ln_b[:], AF.Ln))
                i = s.activation(ln_c[:], ln_c[:], AF.Exp, scale=-0.5)
                p.inc(i, "act", label=f"{tag}_sd")

            def v_fn(v):
                for nt in range(NT):
                    p.wait(v, "pe", label=f"{tag}_stats{nt}")
                    p.chain(v, "dve", v.tensor_copy(ln_a[0:1, nt * 512:(nt + 1) * 512], ps[6][0:1, :]))
                    i = v.tensor_copy(ln_b[0:1, nt * 512:(nt + 1) * 512], ps[7][0:1, :])
                    p.inc(i, "dve", label=f"{tag}_evac{nt}")
                p.wait(v, "dve", val=p.cnt["dve"])
                p.chain(v, "dve", v.tensor_scalar(ln_a[:], ln_a[:], 1.0 / D, 0.0, ALU.mult, ALU.add))
                p.chain(v, "dve", v.tensor_scalar(ln_b[:], ln_b[:], 1.0 / D, 0.0, ALU.mult, ALU.add))
                p.chain(v, "dve", v.tensor_tensor(ln_c[:], ln_a[:], ln_a[:], ALU.mult))
                p.chain(v, "dve", v.tensor_tensor(ln_b[:], ln_b[:], ln_c[:], ALU.subtract))
                i = v.tensor_scalar(ln_b[:], ln_b[:], LN_EPS, 0.0, ALU.add, ALU.add)
                p.inc(i, "dve", label=f"{tag}_var")
                p.wait(v, "act", label=f"{tag}_sd")
                i = v.tensor_tensor(ln_a[:], ln_a[:], ln_c[:], ALU.mult)
                p.inc(i, "dve", label=f"{tag}_rows")
                p.wait(v, "dve", val=p.cnt["dve"])
                for nt in range(NT):
                    p.wait(v, "pe", label=f"{tag}_bc{nt}")
                    for hc in range(HC):
                        p.chain(v, "dve", v.tensor_tensor(ln_st[hc % 2][:],
                                                          x_fm[hc][:, nt * 512:(nt + 1) * 512],
                                                          ps[6][:], ALU.mult))
                        i = v.tensor_tensor(xn[hc][:, nt * 512:(nt + 1) * 512],
                                            ln_st[hc % 2][:], ps[7][:], ALU.subtract)
                    p.inc(i, "dve", label=f"{tag}_norm{nt}")

            p.on(blk, "tensor", t_fn)
            p.on(blk, "scalar", s_fn)
            p.on(blk, "vector", v_fn)

    def router_stage(call):
        tag = f"rt{call}"

        def sc(t_, dims, off=0):
            return _ap(t_, off, dims)

        with p.block() as blk:
            def t_fn(t):
                if call == 0:
                    p.dwait(t, "wload", label=f"{tag}_gwT")
                for tc in range(TC):
                    if tc >= 2:
                        p.wait(t, "act", label=f"{tag}_sig{tc - 2}")
                    for kc in range(HC):
                        i = t.matmul(ps[6 + tc % 2][0:128, 0:64], xn[kc][:, tc * 128:(tc + 1) * 128],
                                     gate_wT[:, kc * 64:(kc + 1) * 64],
                                     start=(kc == 0), stop=(kc == HC - 1))
                    p.inc(i, "pe", label=f"{tag}_lg{tc}")
                p.wait(t, "dve", label=f"{tag}_rw")
                for tc in range(TC):
                    if tc >= 2:
                        p.wait(t, "dve", label=f"{tag}_tev{tc - 2}")
                    i = t.transpose(ps[6 + tc % 2][0:64, 0:128], r_sfc[:, tc * 64:(tc + 1) * 64], iden[:])
                    p.inc(i, "pe", label=f"{tag}_tr{tc}")

            def s_fn(s):
                for tc in range(TC):
                    p.wait(s, "pe", label=f"{tag}_lg{tc}")
                    i = s.activation(r_scores[:, tc * 64:(tc + 1) * 64], ps[6 + tc % 2][0:128, 0:64],
                                     AF.Sigmoid)
                    p.inc(i, "act", label=f"{tag}_sig{tc}")
                if True:
                    p.wait(s, "dve", label=f"{tag}_dn")
                    p.chain(s, "act", s.activation(r_mj[:], r_dn[:], AF.Ln))
                    i = s.activation(r_mj[:], r_mj[:], AF.Exp, scale=-1.0)
                    p.inc(i, "act", label=f"{tag}_rcp")

            def v_fn(v):
                p.wait(v, "act", label=f"{tag}_sig{TC - 1}")
                p.chain(v, "dve", v.tensor_tensor(sc(r_sfc, [[1024, 128], [64, 16], [1, 64]]), sc(r_scores, [[1024, 128], [64, 16], [1, 64]]), sc(gb_bc, [[64, 128], [0, 16], [1, 64]]), ALU.add))
                p.chain(v, "dve", v.tensor_reduce(r_m1[:], sc(r_sfc, [[1024, 128], [8, 128], [1, 8]]), axis=mybir.AxisListType.X, op=ALU.max))
                p.chain(v, "dve", v.tensor_tensor(sc(r_a, [[1024, 128], [8, 128], [1, 8]]), sc(r_sfc, [[1024, 128], [8, 128], [1, 8]]), sc(r_m1, [[128, 128], [1, 128], [0, 8]]), ALU.is_equal))
                p.chain(v, "dve", v.scalar_tensor_tensor(r_b[:], r_a[:], -BIG, r_sfc[:], ALU.mult, ALU.add))
                p.chain(v, "dve", v.tensor_reduce(r_m2[:], sc(r_b, [[1024, 128], [8, 128], [1, 8]]), axis=mybir.AxisListType.X, op=ALU.max))
                p.chain(v, "dve", v.tensor_tensor(r_gs[:], r_m1[:], r_m2[:], ALU.add))
                p.chain(v, "dve", v.tensor_copy(r_gw[:], r_gs[:]))
                p.chain(v, "dve", v.memset(r_gm[:], 0.0))
                for j in range(4):
                    p.chain(v, "dve", v.tensor_reduce(r_mj[:], sc(r_gw, [[128, 128], [8, 16], [1, 8]]), axis=mybir.AxisListType.X, op=ALU.max))
                    p.chain(v, "dve", v.tensor_tensor(sc(r_m1, [[128, 128], [8, 16], [1, 8]]), sc(r_gw, [[128, 128], [8, 16], [1, 8]]), sc(r_mj, [[16, 128], [1, 16], [0, 8]]), ALU.is_equal))
                    p.chain(v, "dve", v.tensor_tensor(r_gm[:], r_gm[:], r_m1[:], ALU.add))
                    p.chain(v, "dve", v.scalar_tensor_tensor(r_gw[:], r_m1[:], -BIG, r_gw[:], ALU.mult, ALU.add))
                p.chain(v, "dve", v.tensor_tensor(sc(r_b, [[1024, 128], [64, 16], [8, 8], [1, 8]]), sc(r_sfc, [[1024, 128], [64, 16], [8, 8], [1, 8]]), sc(r_gm, [[128, 128], [8, 16], [1, 8], [0, 8]]), ALU.mult))
                p.chain(v, "dve", v.memset(r_a[:], 0.0))
                p.chain(v, "dve", v.memset(r_dn[:], 0.0))
                for j in range(8):
                    p.chain(v, "dve", v.tensor_reduce(r_mj[:], sc(r_b, [[1024, 128], [64, 16], [1, 64]]), axis=mybir.AxisListType.X, op=ALU.max))
                    p.chain(v, "dve", v.tensor_tensor(sc(r_sfc, [[1024, 128], [64, 16], [1, 64]]), sc(r_b, [[1024, 128], [64, 16], [1, 64]]), sc(r_mj, [[16, 128], [1, 16], [0, 64]]), ALU.is_equal))
                    p.chain(v, "dve", v.tensor_tensor(r_a[:], r_a[:], r_sfc[:], ALU.add))
                    p.chain(v, "dve", v.tensor_tensor(r_c[:], r_sfc[:], r_scores[:], ALU.mult))
                    p.chain(v, "dve", v.tensor_reduce(r_mj[:], sc(r_c, [[1024, 128], [64, 16], [1, 64]]), axis=mybir.AxisListType.X, op=ALU.max))
                    p.chain(v, "dve", v.tensor_tensor(r_dn[:], r_dn[:], r_mj[:], ALU.add))
                    p.chain(v, "dve", v.scalar_tensor_tensor(r_b[:], r_sfc[:], -BIG, r_b[:], ALU.mult, ALU.add))
                p.chain(v, "dve", v.tensor_tensor(r_a[:], r_a[:], r_scores[:], ALU.mult))
                i = v.tensor_scalar(r_dn[:], r_dn[:], 1.0e-20, 0.0, ALU.add, ALU.add)
                p.inc(i, "dve", label=f"{tag}_dn")
                p.wait(v, "act", label=f"{tag}_rcp")
                p.chain(v, "dve", v.tensor_tensor(sc(r_sfc, [[1024, 128], [64, 16], [1, 64]]), sc(r_a, [[1024, 128], [64, 16], [1, 64]]), sc(r_mj, [[16, 128], [1, 16], [0, 64]]), ALU.mult))
                i = v.tensor_scalar(r_sfc[:], r_sfc[:], RSCALE, 0.0, ALU.mult, ALU.add)
                p.inc(i, "dve", label=f"{tag}_rw")
                for tc in range(TC):
                    p.wait(v, "pe", label=f"{tag}_tr{tc}")
                    i = v.tensor_copy(rwT_sb[0:64, tc * 128:(tc + 1) * 128],
                                      ps[6 + tc % 2][0:64, 0:128])
                    p.inc(i, "dve", label=f"{tag}_tev{tc}")

            def sy_fn(sy):
                if call == 0:
                    p.wait(sy, "pool", label="cast_pro")
                    for kc in range(HC):
                        i = sy.dma_start_transpose(gate_wT[:, kc * 64:(kc + 1) * 64],
                                                   _ap(gwbf, kc * 128, [[D, E], [1, 128]]))
                        p.dinc(i, "wload")
                    p.mark(f"{tag}_gwT", "wload")
                p.wait(sy, "dve", label=f"{tag}_tev{TC - 1}")
                i = sy.dma_start(rwT_d.ap(), rwT_sb[:])
                p.dinc(i, "store", label=f"{tag}_rwT")
                p.dwait(sy, "store", val=p.cnt["store"])

            p.on(blk, "tensor", t_fn)
            p.on(blk, "scalar", s_fn)
            p.on(blk, "vector", v_fn)
            p.on(blk, "sync", sy_fn)

    def ffn_units(call):
        tag = f"f{call}"
        nu = min(N_UNITS, len(UNITS))
        order = [("gu", 0), ("gu", 1), ("d", 0), ("gu", 2), ("d", 1), ("gu", 3), ("d", 2), ("d", 3)]

        with p.block() as blk:
            def sy_fn(sy):
                for ui in range(nu):
                    u = UNITS[ui]
                    iu, mc = u["i"], u["mc"]
                    if ui <= 2:
                        p.wait(sy, "pool", label="cast_pro")
                    else:
                        p.wait(sy, "pool", label=f"castm{ui - 3}")
                    if ui > 0:
                        p.wait(sy, "pe", label=f"{tag}_gdone{ui - 1}")
                    elif call == 1:
                        p.wait(sy, "pe", label=f"f0_gdone{nu - 1}")
                    i = sy.dma_start_transpose(wgT[:], _ap(gbf[ui], 0, [[128, HC * 768], [1, 128]]))
                    p.dinc(i, "wg")
                    p.mark(f"{tag}_gld{ui}", "wg")
                    if ui > 0:
                        p.wait(sy, "pe", label=f"{tag}_udone{ui - 1}")
                    elif call == 1:
                        p.wait(sy, "pe", label=f"f0_udone{nu - 1}")
                    i = sy.dma_start_transpose(wuT[:], _ap(ubf[ui], 0, [[128, HC * 768], [1, 128]]))
                    p.dinc(i, "wu")
                    p.mark(f"{tag}_uld{ui}", "wu")
                    if ui > 0:
                        p.wait(sy, "pe", label=f"{tag}_ddone{ui - 1}")
                    elif call == 1:
                        p.wait(sy, "pe", label=f"f0_ddone{nu - 1}")
                    i = sy.dma_start_transpose(wdT[:], _ap(dbf[ui], 0, [[128, 6 * D], [1, 128]]))
                    p.dinc(i, "wd")
                    p.mark(f"{tag}_dld{ui}", "wd")
                    if u["kind"] == "e":
                        p.dwait(sy, "store", label=f"rt{call}_rwT")
                        lp = None
                        for uj in range(ui - 1, -1, -1):
                            if UNITS[uj]["kind"] == "e":
                                lp = uj
                                break
                        if lp is not None:
                            p.wait(sy, "pool", label=f"{tag}_pl{lp}")
                        elif call == 1:
                            p.wait(sy, "pool", label=f"f0_pl{63 if nu > 63 else nu - 1}")
                        i = sy.dma_start(rw_bc[:], _ap(rwT_d, u["e"] * T, [[0, 128], [1, T]]))
                        p.dinc(i, "wrw")
                        p.mark(f"{tag}_rwld{ui}", "wrw")
                if DEV_UPART == "load":
                    p.dwait(sy, "wload")

            def t_fn(t):
                if DEV_UPART == "load":
                    return
                for ui in range(nu):
                    u = UNITS[ui]
                    mc, lc = u["mc"], u["lc"]
                    for kind, nt in order:
                        if kind == "d" and DEV_UPART == "gu":
                            continue
                        if kind == "gu":
                            for m in range(mc):
                                mw = lc if m == mc - 1 else 128
                                j = GCH[(call, ui, nt, m)]
                                if nt == 0 and m == 0:
                                    p.dwait(t, "wg", label=f"{tag}_gld{ui}")
                                if j >= 2:
                                    p.wait(t, "act", label=f"{tag}_silu{j - 2}")
                                for kc in range(HC):
                                    i = t.matmul(ps[j % 2][0:mw, :],
                                                 wgT[:, kc * 768 + m * 128: kc * 768 + m * 128 + mw],
                                                 xn[kc][:, nt * 512:(nt + 1) * 512],
                                                 start=(kc == 0), stop=(kc == HC - 1))
                                p.inc(i, "pe", label=f"{tag}_g{j}")
                                if nt == 0 and m == 0:
                                    p.dwait(t, "wu", label=f"{tag}_uld{ui}")
                                if j >= 2:
                                    p.wait(t, "dve", label=f"{tag}_mult{j - 2}")
                                for kc in range(HC):
                                    i = t.matmul(ps[2 + j % 2][0:mw, :],
                                                 wuT[:, kc * 768 + m * 128: kc * 768 + m * 128 + mw],
                                                 xn[kc][:, nt * 512:(nt + 1) * 512],
                                                 start=(kc == 0), stop=(kc == HC - 1))
                                p.inc(i, "pe", label=f"{tag}_u{j}")
                            if nt == 3:
                                jl = GCH[(call, ui, 3, mc - 1)]
                                if p.dry:
                                    p.marks[f"{tag}_gdone{ui}"] = p.marks[f"{tag}_g{jl}"]
                                    p.marks[f"{tag}_udone{ui}"] = p.marks[f"{tag}_u{jl}"]
                        else:
                            if nt == 0:
                                p.dwait(t, "wd", label=f"{tag}_dld{ui}")
                            if u["kind"] == "e":
                                p.wait(t, "pool", label=f"{tag}_hact{ui}_{nt}")
                            else:
                                p.wait(t, "dve", label=f"{tag}_hact{ui}_{nt}")
                            for hc in range(HC):
                                dj = DCH[(call, ui, nt, hc)]
                                if dj >= 2:
                                    p.wait(t, "dve", label=f"{tag}_dev{dj - 2}")
                                for ic in range(mc):
                                    kw = lc if ic == mc - 1 else 128
                                    i = t.matmul(ps[4 + dj % 2][:, :],
                                                 wdT[0:kw, ic * 1024 + hc * 128: ic * 1024 + hc * 128 + 128],
                                                 hact[nt % 2][0:kw, ic * 512:(ic + 1) * 512],
                                                 start=(ic == 0), stop=(ic == mc - 1))
                                p.inc(i, "pe", label=f"{tag}_d{dj}")
                            p.mark(f"{tag}_dcol{ui}_{nt}", "pe")
                            if nt == 3:
                                p.mark(f"{tag}_ddone{ui}", "pe")

            def s_fn(s):
                if DEV_UPART == "load":
                    return
                for ui in range(nu):
                    u = UNITS[ui]
                    mc, lc = u["mc"], u["lc"]
                    for nt in range(NT):
                        for m in range(mc):
                            mw = lc if m == mc - 1 else 128
                            j = GCH[(call, ui, nt, m)]
                            p.wait(s, "pe", label=f"{tag}_g{j}")
                            if j >= 2:
                                p.wait(s, "dve", label=f"{tag}_mult{j - 2}")
                            i = s.activation(sg_st[j % 2][0:mw, :], ps[j % 2][0:mw, :], AF.Silu)
                            p.inc(i, "act", label=f"{tag}_silu{j}")

            def v_fn(v):
                if DEV_UPART == "load":
                    return
                for ui in range(nu):
                    u = UNITS[ui]
                    mc, lc = u["mc"], u["lc"]
                    routed = u["kind"] == "e"
                    for kind, nt in order:
                        if kind == "d" and DEV_UPART == "gu":
                            continue
                        if kind == "gu":
                            for m in range(mc):
                                mw = lc if m == mc - 1 else 128
                                j = GCH[(call, ui, nt, m)]
                                p.wait(v, "pe", label=f"{tag}_u{j}")
                                p.wait(v, "act", label=f"{tag}_silu{j}")
                                if routed:
                                    if j >= 2:
                                        p.wait(v, "pool", label=f"{tag}_rwm{j - 2}")
                                    i = v.tensor_tensor(sgu_st[j % 2][0:mw, :], sg_st[j % 2][0:mw, :],
                                                        ps[2 + j % 2][0:mw, :], ALU.mult)
                                    p.inc(i, "dve", label=f"{tag}_mult{j}")
                                else:
                                    gidx = ui * 4 + nt
                                    if gidx >= 2:
                                        ui2, nt2 = (gidx - 2) // 4, (gidx - 2) % 4
                                        p.wait(v, "pe", label=f"{tag}_dcol{ui2}_{nt2}")
                                    i = v.tensor_tensor(hact[nt % 2][0:mw, m * 512:(m + 1) * 512],
                                                        sg_st[j % 2][0:mw, :],
                                                        ps[2 + j % 2][0:mw, :], ALU.mult)
                                    c = p.inc(i, "dve", label=f"{tag}_mult{j}")
                                    if p.dry and m == mc - 1:
                                        p.marks[f"{tag}_hact{ui}_{nt}"] = c
                        else:
                            for hc in range(HC):
                                dj = DCH[(call, ui, nt, hc)]
                                p.wait(v, "pe", label=f"{tag}_d{dj}")
                                i = v.scalar_tensor_tensor(x_fm[hc][:, nt * 512:(nt + 1) * 512],
                                                           ps[4 + dj % 2][:, :], 0.5,
                                                           x_fm[hc][:, nt * 512:(nt + 1) * 512],
                                                           ALU.mult, ALU.add)
                                p.inc(i, "dve", label=f"{tag}_dev{dj}")

            def g_fn(g):
                if DEV_UPART == "load":
                    return
                for ui in range(nu):
                    u = UNITS[ui]
                    casted = False
                    if call == 0 and ui + 3 < nu:
                        emit_unit_casts(g, ui + 3)
                        casted = True
                    if u["kind"] != "e":
                        if casted:
                            p.dwait(g, "cast", val=p.cnt["cast"])
                            i = g.memset(pdum[:], 0.0)
                            p.inc(i, "pool", label=f"castm{ui}")
                        continue
                    mc, lc = u["mc"], u["lc"]
                    p.dwait(g, "wrw", label=f"{tag}_rwld{ui}")
                    for nt in range(NT):
                        gidx = ui * 4 + nt
                        if gidx >= 2:
                            ui2, nt2 = (gidx - 2) // 4, (gidx - 2) % 4
                            p.wait(g, "pe", label=f"{tag}_dcol{ui2}_{nt2}")
                        for m in range(mc):
                            mw = lc if m == mc - 1 else 128
                            j = GCH[(call, ui, nt, m)]
                            p.wait(g, "dve", label=f"{tag}_mult{j}")
                            i = g.tensor_tensor(hact[nt % 2][0:mw, m * 512:(m + 1) * 512],
                                                sgu_st[j % 2][0:mw, :],
                                                rw_bc[0:mw, nt * 512:(nt + 1) * 512], ALU.mult)
                            c = p.inc(i, "pool", label=f"{tag}_rwm{j}")
                            if p.dry and m == mc - 1:
                                p.marks[f"{tag}_hact{ui}_{nt}"] = c
                    if casted:
                        p.dwait(g, "cast", val=p.cnt["cast"])
                        i = g.memset(pdum[:], 0.0)
                        p.inc(i, "pool", label=f"castm{ui}")
                    p.mark(f"{tag}_pl{ui}", "pool")

            p.on(blk, "sync", sy_fn)
            p.on(blk, "tensor", t_fn)
            p.on(blk, "scalar", s_fn)
            p.on(blk, "vector", v_fn)
            p.on(blk, "gpsimd", g_fn)

    def attn_consts():
        with p.block() as blk:
            def sy_fn(sy):
                p.dinc(sy.dma_start(cos_sb[:], cos_c), "wload")
                p.dinc(sy.dma_start(sin_sb[:], sin_c), "wload")
                p.dinc(sy.dma_start(mask_sb[:], mask_c), "wload")
                p.mark("at_consts", "wload")
            p.on(blk, "sync", sy_fn)

            def g_fn(g):
                p.dinc(g.dma_start(vb_row[:], _ap(vb_i.tensor, 0, [[D, 1], [1, D]])), "cast",
                       label="at_vb")
                i = g.memset(v_tm[:], 1.0)
                p.inc(i, "pool", label="at_vones")
            p.on(blk, "gpsimd", g_fn)

    def attn_pass(hg):
        tag = "at"
        with p.block() as blk:
            def g_fn(g):
                for wi, wt in enumerate([wq, wk, wv, wo]):
                    if wi == 0:
                        if hg == 1:
                            p.wait(g, "pe", label=f"{tag}_wfree1_0")
                    else:
                        p.wait(g, "pe", label=f"{tag}_wfree{hg}_{wi}")
                    if wi < 3:
                        src = _ap(wt.tensor, 512 * hg, [[D, 128], [128 * D, HC], [1, 512]])
                        dst = _ap(whalf, 0, [[4096, 128], [512, HC], [1, 512]])
                    else:
                        src = _ap(wt.tensor, 512 * hg * D, [[D, 128], [128 * D, 4], [1, 1024]])
                        dst = _ap(whalf, 0, [[4096, 128], [1024, 4], [1, 1024]])
                    p.dinc(g.dma_start(dst, src), "cast", label=f"{tag}_w{hg}_{wi}")

            def t_fn(t):
                cc = 0
                for wi in range(3):
                    p.dwait(t, "cast", label=f"{tag}_w{hg}_{wi}")
                    if wi < 2:
                        for ocl in range(4):
                            for tn in range(NT):
                                if cc >= 2:
                                    p.wait(t, "dve", label=f"{tag}_pj{hg}_{cc - 2}")
                                for kc in range(HC):
                                    i = t.matmul(ps[4 + cc % 2][:, :],
                                                 _ap(whalf, kc * 512 + ocl * 128, [[4096, 128], [1, 128]]),
                                                 xn[kc][:, tn * 512:(tn + 1) * 512],
                                                 start=(kc == 0), stop=(kc == HC - 1))
                                p.inc(i, "pe", label=f"{tag}_pc{hg}_{cc}")
                                cc += 1
                    else:
                        p.wait(t, "pool", label="ones_ready")
                        p.dwait(t, "wload", label="at_consts")
                        for tc in range(TC):
                            if cc >= 34:
                                p.wait(t, "act", label=f"{tag}_vev{hg}_{cc - 2}")
                            elif cc >= 2:
                                p.wait(t, "dve", label=f"{tag}_pj{hg}_{cc - 2}")
                            for kc in range(HC):
                                t.matmul(ps[4 + cc % 2][:, :],
                                         xn[kc][:, tc * 128:(tc + 1) * 128],
                                         _ap(whalf, kc * 512, [[4096, 128], [1, 512]]),
                                         start=(kc == 0), stop=False)
                            i = t.matmul(ps[4 + cc % 2][:, :], ones_rowb[:],
                                         _ap(vb_row, 512 * hg, [[D, 1], [1, 512]]),
                                         start=False, stop=True)
                            p.inc(i, "pe", label=f"{tag}_pc{hg}_{cc}")
                            cc += 1
                    p.mark(f"{tag}_wfree{hg}_{wi + 1}", "pe")
                if DEV_APART == "qkv":
                    return
                rp = 0
                for qk, src in enumerate((qh, kh)):
                    for c in range(4):
                        for tn in range(NT):
                            p.wait(t, "dve", label=f"{tag}_pj{hg}_{(0 if qk == 0 else 16) + c * 4 + tn}")
                            if rp >= 2:
                                p.wait(t, "dve", label=f"{tag}_rt1_{hg}_{rp - 2}")
                            i = t.matmul(ps[6 + rp % 2][:, :], perm[:],
                                         src[c][:, tn * 512:(tn + 1) * 512], start=True, stop=True)
                            p.inc(i, "pe", label=f"{tag}_perm{hg}_{rp}")
                            rp += 1
                if DEV_APART == "rope":
                    return
                p.wait(t, "dve", label=f"{tag}_rt1_{hg}_{31}")
                p.wait(t, "dve", label=f"{tag}_rt1_{hg}_{30}")
                for c in range(4):
                    for tn in range(NT):
                        for qk in range(2):
                            p.wait(t, "act", label=f"{tag}_bsq{hg}_{c}_{tn}_{qk}")
                            i = t.matmul(ps[7][0:1, 0:512], ones_col[:], sqst[:],
                                         start=True, stop=True)
                            p.inc(i, "pe", label=f"{tag}_bmm{hg}_{c}_{tn}_{qk}")
                            p.wait(t, "dve", label=f"{tag}_bmx{hg}_{c}_{tn}_{qk}")
                p.wait(t, "dve", label=f"{tag}_nbv{hg}")
                for c in range(4):
                    i = t.matmul(ps[7][:, 0:1], ones_row[:], nmax[0:1, 8 + c:9 + c],
                                 start=True, stop=True)
                    p.inc(i, "pe", label=f"{tag}_nbb{hg}_{c}")
                    p.wait(t, "act", label=f"{tag}_nbc{hg}_{c}")
                if DEV_APART == "bound":
                    return
                sti = 0
                cgi = 0
                last_odd = [None]
                for hl in range(8):
                    c, rb = hl // 2, 64 * (hl % 2)
                    odd = hl % 2 == 1
                    for qb in range(NT):
                        kmax = 4 * qb + 4

                        def ctx_mm(kcp, last):
                            sip = sti + kcp
                            if kcp == 0 and cgi >= 1:
                                p.wait(t, "act", label=f"{tag}_cst{hg}_{cgi - 1}")
                            if odd and kcp == 0 and last_odd[0] is not None:
                                p.wait(t, "act", label=f"{tag}_rsx{hg}_{last_odd[0]}")
                            diag = kcp >= 4 * qb
                            p.wait(t, "dve" if diag else "act",
                                   label=(f"{tag}_msk{hg}_{sip}" if diag else f"{tag}_exp{hg}_{sip}"))
                            if not odd:
                                i = t.matmul(ps[2][0:65, :],
                                             _ap(v_tm, kcp * 520 + 65 * hl, [[520 * 16, 128], [1, 65]]),
                                             attnT[sip % 2][:], start=(kcp == 0), stop=last)
                                p.inc(i, "pe", label=f"{tag}_ctx{hg}_{sip}")
                            else:
                                i = t.matmul(ps[6][0:1, :], ones_colb[:, 0:1],
                                             attnT[sip % 2][:], start=(kcp == 0), stop=last)
                                if last:
                                    p.inc(i, "pe", label=f"{tag}_sum{hg}_{sip}")
                                i = t.matmul(ps[2][64:128, :],
                                             _ap(v_tm, kcp * 520 + 65 * hl, [[520 * 16, 128], [1, 64]]),
                                             attnT[sip % 2][:], start=(kcp == 0), stop=last,
                                             tile_position=(0, 64))
                                p.inc(i, "pe", label=f"{tag}_ctx{hg}_{sip}")

                        for kc in range(kmax):
                            si = sti + kc
                            if si >= 2:
                                p.wait(t, "act", label=f"{tag}_exp{hg}_{si - 2}")
                            i = t.matmul(ps[si % 2][:, :],
                                         kh[c][rb:rb + 64, kc * 128:(kc + 1) * 128],
                                         qh[c][rb:rb + 64, qb * 512:(qb + 1) * 512],
                                         start=True, stop=True)
                            p.inc(i, "pe", label=f"{tag}_sT{hg}_{si}")
                            if kc >= 1:
                                ctx_mm(kc - 1, kc - 1 == kmax - 1)
                        ctx_mm(kmax - 1, True)
                        sti += kmax
                        p.wait(t, "act", label=f"{tag}_rsx{hg}_{cgi}")
                        if cgi >= 1:
                            p.wait(t, "dve", label=f"{tag}_ctxev{hg}_{cgi - 1}")
                        i = t.matmul(ps[3][64:128, :] if odd else ps[3][0:64, :],
                                     ones_rowb[0:1, 0:64], rs_bf[0:1, 0:512], start=True, stop=True,
                                     tile_position=(0, 64) if odd else None)
                        p.inc(i, "pe", label=f"{tag}_rb{hg}_{cgi}")
                        if odd:
                            last_odd[0] = cgi
                        cgi += 1
                if DEV_APART in ("head", "h1", "h2", "h2a", "h2b"):
                    return
                p.dwait(t, "cast", label=f"{tag}_w{hg}_3")
                p.wait(t, "dve", label=f"{tag}_ctxev{hg}_{cgi - 1}")
                oj = 0
                for oc in range(8):
                    for tn in range(NT):
                        if oj >= 2:
                            p.wait(t, "dve", label=f"{tag}_oev{hg}_{oj - 2}")
                        for kcl in range(4):
                            i = t.matmul(ps[4 + oj % 2][:, :],
                                         _ap(whalf, kcl * 1024 + oc * 128, [[4096, 128], [1, 128]]),
                                         ctxh[kcl][:, tn * 512:(tn + 1) * 512],
                                         start=(kcl == 0), stop=(kcl == 3))
                        p.inc(i, "pe", label=f"{tag}_oc{hg}_{oj}")
                        oj += 1
                p.mark(f"{tag}_wfree{hg + 1}_0", "pe")

            def s_fn(s):
                for tc in range(TC):
                    cc = 32 + tc
                    p.wait(s, "pe", label=f"{tag}_pc{hg}_{cc}")
                    if tc >= 1:
                        p.wait(s, "dve", label=f"{tag}_pj{hg}_{cc - 1}")
                    i = s.copy(v_st[:], ps[4 + cc % 2][:, :])
                    p.inc(i, "act", label=f"{tag}_vev{hg}_{cc}")
                if DEV_APART == "qkv":
                    return
                if DEV_APART == "rope":
                    return
                first_sq = [True]
                for c in range(4):
                    for tn in range(NT):
                        for qk, src in enumerate((qh, kh)):
                            p.wait(s, "dve", label=f"{tag}_rfin{hg}_{qk}_{c}_{tn}")
                            p.wait(s, "pe", label=f"{tag}_bprev{hg}_{c}_{tn}_{qk}")
                            if first_sq[0]:
                                p.wait(s, "dve", label=f"{tag}_pj{hg}_47")
                                first_sq[0] = False
                            i = s.square(sqst[:], src[c][:, tn * 512:(tn + 1) * 512])
                            p.inc(i, "act", label=f"{tag}_bsq{hg}_{c}_{tn}_{qk}")
                p.wait(s, "dve", label=f"{tag}_nmaxv{hg}")
                i = s.activation(nmax[0:1, 0:8], nmax[0:1, 0:8], AF.Sqrt)
                p.inc(i, "act", label=f"{tag}_nsq{hg}")
                for c in range(4):
                    p.wait(s, "pe", label=f"{tag}_nbb{hg}_{c}")
                    i = s.copy(nb_sb[c][:], ps[7][:, 0:1])
                    p.inc(i, "act", label=f"{tag}_nbc{hg}_{c}")
                if DEV_APART == "bound":
                    return
                sti = 0
                cgi = 0
                for hl in range(8):
                    c = hl // 2
                    odd = hl % 2 == 1
                    for qb in range(NT):
                        kmax = 4 * qb + 4
                        for kc in range(kmax):
                            p.wait(s, "pe", label=f"{tag}_sT{hg}_{sti}")
                            if sti >= 2:
                                p.wait(s, "pe", label=f"{tag}_ctx{hg}_{sti - 2}")
                            i = s.activation(attnT[sti % 2][:], ps[sti % 2][:], AF.Exp,
                                             bias=nb_sb[c][:, 0:1], scale=0.125)
                            p.inc(i, "act", label=f"{tag}_exp{hg}_{sti}")
                            sti += 1
                        p.wait(s, "pe", label=f"{tag}_ctx{hg}_{sti - 1}")
                        if not odd:
                            c_ = p.inc(s.copy(ctx_st[0:65, :], ps[2][0:65, :]),
                                       "act", label=f"{tag}_cst{hg}_{cgi}")
                            if not p.dry:
                                s.wait_ge(p.sem["act"], c_)
                            p.chain(s, "act", s.activation(rs_sb[0:1, :], ctx_st[64:65, :], AF.Ln))
                        else:
                            p.wait(s, "pe", label=f"{tag}_sum{hg}_{sti - 1}")
                            c_ = p.inc(s.copy(ctx_st[64:128, :], ps[2][64:128, :]),
                                       "act", label=f"{tag}_cst{hg}_{cgi}")
                            if not p.dry:
                                s.wait_ge(p.sem["act"], c_)
                            p.chain(s, "act", s.activation(rs_sb[0:1, :], ps[6][0:1, :], AF.Ln))
                        i = s.activation(rs_bf[0:1, 0:512], rs_sb[0:1, :], AF.Exp, scale=-1.0)
                        p.inc(i, "act", label=f"{tag}_rsx{hg}_{cgi}")
                        cgi += 1

            def v_fn(v):
                cc = 0
                for wi in range(3):
                    if wi < 2:
                        dstl = qh if wi == 0 else kh
                        bias = qb_pp if wi == 0 else kb_pp
                        for ocl in range(4):
                            for tn in range(NT):
                                p.wait(v, "pe", label=f"{tag}_pc{hg}_{cc}")
                                i = v.tensor_scalar(dstl[ocl][:, tn * 512:(tn + 1) * 512],
                                                    ps[4 + cc % 2][:, :],
                                                    bias[:, 4 * hg + ocl:4 * hg + ocl + 1], 0.0,
                                                    ALU.add, ALU.add)
                                p.inc(i, "dve", label=f"{tag}_pj{hg}_{cc}")
                                cc += 1
                    else:
                        p.wait(v, "pool", label="at_vones")
                        for tc in range(TC):
                            p.wait(v, "act", label=f"{tag}_vev{hg}_{cc}")
                            i = v.tensor_copy(_ap(v_tm, tc * 520, [[520 * 16, 128], [65, 8], [1, 64]]),
                                              v_st[:])
                            p.inc(i, "dve", label=f"{tag}_pj{hg}_{cc}")
                            cc += 1
                if DEV_APART == "qkv":
                    return
                rp = 0
                for qk, src in enumerate((qh, kh)):
                    for c in range(4):
                        for tn in range(NT):
                            p.chain(v, "dve", v.tensor_tensor(rope_t0[:],
                                              src[c][:, tn * 512:(tn + 1) * 512],
                                              cos_sb[:, tn * 512:(tn + 1) * 512], ALU.mult))
                            p.wait(v, "pe", label=f"{tag}_perm{hg}_{rp}")
                            i = v.tensor_tensor(rope_t1[:], ps[6 + rp % 2][:, :],
                                                sin_sb[:, tn * 512:(tn + 1) * 512], ALU.mult)
                            c_ = p.inc(i, "dve", label=f"{tag}_rt1_{hg}_{rp}")
                            if not p.dry:
                                v.wait_ge(p.sem["dve"], c_)
                            i = v.tensor_tensor(src[c][:, tn * 512:(tn + 1) * 512],
                                                rope_t0[:], rope_t1[:], ALU.add)
                            p.inc(i, "dve", label=f"{tag}_rfin{hg}_{qk}_{c}_{tn}")
                            p.wait(v, "dve", val=p.cnt["dve"])
                            rp += 1
                if DEV_APART == "rope":
                    return
                for c in range(4):
                    for tn in range(NT):
                        for qk in range(2):
                            p.wait(v, "pe", label=f"{tag}_bmm{hg}_{c}_{tn}_{qk}")
                            col = qk * 4 + c
                            if tn == 0:
                                i = v.tensor_reduce(nmax[0:1, col:col + 1], ps[7][0:1, 0:512],
                                                    axis=mybir.AxisListType.X, op=ALU.max)
                            else:
                                p.chain(v, "dve", v.tensor_reduce(nmax[0:1, 15:16], ps[7][0:1, 0:512],
                                                axis=mybir.AxisListType.X, op=ALU.max))
                                i = v.tensor_tensor(nmax[0:1, col:col + 1], nmax[0:1, col:col + 1],
                                                    nmax[0:1, 15:16], ALU.max)
                            p.inc(i, "dve", label=f"{tag}_bmx{hg}_{c}_{tn}_{qk}")
                p.mark(f"{tag}_nmaxv{hg}", "dve")
                p.wait(v, "act", label=f"{tag}_nsq{hg}")
                p.chain(v, "dve", v.tensor_tensor(nmax[0:1, 8:12], nmax[0:1, 0:4], nmax[0:1, 4:8], ALU.mult))
                i = v.tensor_scalar(nmax[0:1, 8:12], nmax[0:1, 8:12], -0.125, 0.0, ALU.mult, ALU.add)
                p.inc(i, "dve", label=f"{tag}_nbv{hg}")
                if DEV_APART == "bound":
                    return
                sti = 0
                cgi = 0
                for hl in range(8):
                    c, rb = hl // 2, 64 * (hl % 2)
                    for qb in range(NT):
                        kmax = 4 * qb + 4
                        for kc in range(kmax):
                            if kc >= 4 * qb:
                                rel = kc - 4 * qb
                                p.wait(v, "act", label=f"{tag}_exp{hg}_{sti}")
                                i = v.tensor_tensor(attnT[sti % 2][:], attnT[sti % 2][:],
                                                    mask_sb[:, rel * 512:(rel + 1) * 512], ALU.mult)
                                p.inc(i, "dve", label=f"{tag}_msk{hg}_{sti}")
                            sti += 1
                        p.wait(v, "act", label=f"{tag}_cst{hg}_{cgi}")
                        p.wait(v, "pe", label=f"{tag}_rb{hg}_{cgi}")
                        if hl % 2 == 0:
                            i = v.tensor_tensor(ctxh[c][0:64, qb * 512:(qb + 1) * 512],
                                                ctx_st[0:64, :], ps[3][0:64, :], ALU.mult)
                        else:
                            i = v.tensor_tensor(ctxh[c][64:128, qb * 512:(qb + 1) * 512],
                                                ctx_st[64:128, :], ps[3][64:128, :], ALU.mult)
                        p.inc(i, "dve", label=f"{tag}_ctxev{hg}_{cgi}")
                        cgi += 1
                if DEV_APART in ("head", "h1", "h2", "h2a", "h2b"):
                    return
                for oj in range(32):
                    oc, tn = oj // 4, oj % 4
                    p.wait(v, "pe", label=f"{tag}_oc{hg}_{oj}")
                    if hg == 0:
                        i = v.scalar_tensor_tensor(x_fm[oc][:, tn * 512:(tn + 1) * 512],
                                                   ps[4 + oj % 2][:, :], ob_pp[:, oc:oc + 1],
                                                   x_fm[oc][:, tn * 512:(tn + 1) * 512],
                                                   ALU.add, ALU.add)
                    else:
                        i = v.tensor_tensor(x_fm[oc][:, tn * 512:(tn + 1) * 512],
                                            x_fm[oc][:, tn * 512:(tn + 1) * 512],
                                            ps[4 + oj % 2][:, :], ALU.add)
                    p.inc(i, "dve", label=f"{tag}_oev{hg}_{oj}")

            p.on(blk, "gpsimd", g_fn)
            p.on(blk, "tensor", t_fn)
            p.on(blk, "scalar", s_fn)
            p.on(blk, "vector", v_fn)

    def register_bprev():
        for hg in range(2):
            prev = None
            for c in range(4):
                for tn in range(NT):
                    for qk in range(2):
                        lab = f"at_bprev{hg}_{c}_{tn}_{qk}"
                        if prev is None:
                            p.marks[lab] = p.marks.get(f"at_bmm{1 - hg}_3_3_1", 0) if hg == 1 else 0
                        else:
                            p.marks[lab] = p.marks.get(f"at_bmm{hg}_{prev[0]}_{prev[1]}_{prev[2]}", 0)
                        prev = (c, tn, qk)

    def output_stage():
        with p.block() as blk:
            def sy_fn(sy):
                p.dwait(sy, "wload", label="prologue_loads")
                for c in range(HC):
                    i = sy.dma_start(y_out[c * 128:(c + 1) * 128, :], x_fm[c][:])
                    p.dinc(i, "store")
                if DEV_STAGE:
                    for c in range(HC):
                        p.dinc(sy.dma_start(xn_out[c * 128:(c + 1) * 128, :], xn[c][:]), "store")
                    if DEV_STAGE == "rt":
                        p.dinc(sy.dma_start(rw_out.tensor.ap(), rwT_sb[:]), "store")
                        p.dinc(sy.dma_start(dn_out, r_dn[:]), "store")
                        p.dinc(sy.dma_start(rcp_out, r_mj[:]), "store")
                        p.dinc(sy.dma_start(sco_out, r_scores[:]), "store")
                        p.dinc(sy.dma_start(sel_out, r_a[:]), "store")
                p.dwait(sy, "store")
            p.on(blk, "sync", sy_fn)

    def emit_all():
        prologue()
        if DO_FFN1:
            ln_stage(0)
            if DEV_STAGE != "ln":
                router_stage(0)
                if DEV_STAGE != "rt":
                    ffn_units(0)
        if DO_ATTN:
            ln_stage(2)
            attn_consts()
            attn_pass(0)
            attn_pass(1)
        if DO_FFN2:
            ln_stage(1)
            router_stage(1)
            ffn_units(1)
        output_stage()

    p.start(dry=True)
    emit_all()
    register_bprev()
    final_cnt = dict(p.cnt)
    p.start(dry=False)
    emit_all()
    assert final_cnt == p.cnt, (final_cnt, p.cnt)
    return nc



def _make_consts():
    import ml_dtypes
    j = np.arange(128) % 32
    ts = float(10000.0) ** (2.0 * j / HD)
    pos = np.arange(T)
    ang = pos[None, :] / ts[:, None]
    cos_t = np.cos(ang).astype(ml_dtypes.bfloat16)
    sin_t = np.sin(ang).astype(ml_dtypes.bfloat16)
    perm = np.zeros((128, 128), np.float32)
    for r in range(128):
        b = (r // 64) * 64
        jj = r - b
        if jj < 32:
            perm[b + jj + 32, r] = -1.0
        else:
            perm[b + jj - 32, r] = 1.0
    perm = perm.astype(ml_dtypes.bfloat16)
    masks = np.zeros((128, 4 * 512), np.float32)
    for ri, rel in enumerate((0, 128, 256, 384)):
        k = np.arange(128)[:, None] + rel
        q = np.arange(512)[None, :]
        masks[:, ri * 512:(ri + 1) * 512] = (k <= q).astype(np.float32)
    masks = masks.astype(ml_dtypes.bfloat16)
    iden = np.eye(128, dtype=np.float32)
    return cos_t, sin_t, perm, masks, iden


def _pad_gu(W, S):
    W = np.asarray(W, np.float32)
    S = np.asarray(S, np.float32)
    out = np.zeros((66, 768, D), np.float32)
    out[:64, :I_EXP, :] = W
    out[64, :768, :] = S[0:768]
    out[65, :640, :] = S[768:1408]
    return out


def _pad_d(W, S):
    W = np.asarray(W, np.float32)
    S = np.asarray(S, np.float32)
    out = np.zeros((66, D, 768), np.float32)
    out[:64, :, :I_EXP] = W
    out[64, :, :768] = S[:, 0:768]
    out[65, :, :640] = S[:, 768:1408]
    return out


def kernel(L0_attn_x, L0_attn_wq_w, L0_attn_wq_b, L0_attn_wk_w, L0_attn_wk_b,
           L0_attn_wv_w, L0_attn_wv_b, L0_attn_wo_w, L0_attn_wo_b,
           L0_ffn_gate_w, L0_ffn_gate_bias, L0_ffn_Wg, L0_ffn_Wu, L0_ffn_Wd,
           L0_ffn_Sg, L0_ffn_Su, L0_ffn_Sd):
    cos_t, sin_t, perm, masks, iden = _make_consts()
    nc = build_nc()
    common = dict(
        wq=np.ascontiguousarray(np.asarray(L0_attn_wq_w, np.float32)),
        wk=np.ascontiguousarray(np.asarray(L0_attn_wk_w, np.float32)),
        wv=np.ascontiguousarray(np.asarray(L0_attn_wv_w, np.float32)),
        wo=np.ascontiguousarray(np.asarray(L0_attn_wo_w, np.float32)),
        qb=np.asarray(L0_attn_wq_b, np.float32), kb=np.asarray(L0_attn_wk_b, np.float32),
        vb=np.asarray(L0_attn_wv_b, np.float32), ob=np.asarray(L0_attn_wo_b, np.float32),
        gate_w=np.ascontiguousarray(np.asarray(L0_ffn_gate_w, np.float32)),
        gate_b=np.asarray(L0_ffn_gate_bias, np.float32),
        Wg=_pad_gu(L0_ffn_Wg, L0_ffn_Sg), Wu=_pad_gu(L0_ffn_Wu, L0_ffn_Su),
        Wd=_pad_d(L0_ffn_Wd, L0_ffn_Sd),
        cos_t=cos_t, sin_t=sin_t, perm=perm, masks=masks, iden=iden,
    )
    x = np.asarray(L0_attn_x, np.float32)
    in_maps = []
    for c in range(8):
        m = dict(common)
        m["x_fm"] = np.ascontiguousarray(x[c].T)
        in_maps.append(m)
    res = run_bass_kernel_spmd(nc, in_maps, core_ids=list(range(8)))
    out = np.stack([np.asarray(r["y_fm"]).T for r in res.results], axis=0)
    return np.ascontiguousarray(out.astype(np.float32))


# version 03 (final)
# speedup vs baseline: 1.1911x; 1.0539x over previous
import numpy as np
import concourse.bass as bass
import concourse.mybir as mybir
from concourse.bass_utils import run_bass_kernel_spmd

F32 = mybir.dt.float32
BF16 = mybir.dt.bfloat16
AF = mybir.ActivationFunctionType
ALU = mybir.AluOpType

D = 1024
T = 2048
HC = 8
NT = 4
TC = 16
NH = 16
HD = 64
E = 64
I_EXP = 704
BIG = 1.0e30
LN_EPS = 1.0e-5
RSCALE = 2.5

UNITS = [dict(kind="e", e=e, i=I_EXP) for e in range(E)] + [
    dict(kind="sa", i=768), dict(kind="sb", i=640)]
for u in UNITS:
    u["mc"] = (u["i"] + 127) // 128
    u["lc"] = u["i"] - 128 * (u["mc"] - 1)

N_UNITS = 66
DO_FFN1 = True
DO_ATTN = True
DO_FFN2 = True
DEV_STAGE = ""
DEV_UPART = ""
DEV_APART = ""


def _ap(t, off, dims):
    return bass.AP(t, off, dims)


class _MockInstr:
    def then_inc(self, *a, **k):
        return self


class _MockEng:
    def wait_ge(self, *a, **k):
        return None

    def __getattr__(self, name):
        return lambda *a, **k: _MockInstr()


class _MockBlock:
    def __enter__(self):
        return self

    def __exit__(self, *a):
        return False


class P:

    def __init__(self, nc):
        self.nc = nc
        self.sem = {}
        for s in ["cast", "wload", "wg0", "wg1", "wu0", "wu1", "wd0", "wd1", "wrw0", "wrw1",
                  "store", "pe", "act", "dve", "pool"]:
            self.sem[s] = nc.semaphore(s).__enter__()
        self.marks = {}
        self.dry = True
        self.cnt = {}

    def start(self, dry):
        self.dry = dry
        self.cnt = {k: 0 for k in self.sem}

    def block(self):
        return _MockBlock() if self.dry else self.nc.Block()

    def on(self, blk, eng, fn):
        if self.dry:
            fn(_MockEng())
        else:
            getattr(blk, eng)(fn)

    def mark(self, label, s):
        if self.dry:
            self.marks[label] = self.cnt[s]

    def inc(self, instr, s, label=None, n=1):
        self.cnt[s] += n
        if not self.dry:
            instr.then_inc(self.sem[s], n)
        if label is not None:
            if self.dry:
                self.marks[label] = self.cnt[s]
            else:
                assert self.marks[label] == self.cnt[s], (label, self.marks[label], self.cnt[s])
        return self.cnt[s]

    def dinc(self, instr, s, label=None):
        self.cnt[s] += 1
        if not self.dry:
            instr.then_inc(self.sem[s], 16)
        if label is not None:
            if self.dry:
                self.marks[label] = self.cnt[s]
            else:
                assert self.marks[label] == self.cnt[s], label
        return self.cnt[s]

    STRICT = False

    def chain(self, eng, s, instr):
        c = self.inc(instr, s)
        if not self.dry:
            eng.wait_ge(self.sem[s], c)
        return c

    def wait(self, eng, s, label=None, val=None):
        if self.dry:
            return
        if val is None:
            if label is not None and label not in self.marks and P.STRICT:
                raise AssertionError(f"missing mark: {label}")
            val = self.marks.get(label, 0) if label is not None else self.cnt[s]
        if val <= 0:
            return
        eng.wait_ge(self.sem[s], val)

    def dwait(self, eng, s, label=None, val=None):
        if self.dry:
            return
        if val is None:
            if label is not None and label not in self.marks and P.STRICT:
                raise AssertionError(f"missing mark: {label}")
            val = self.marks.get(label, 0) if label is not None else self.cnt[s]
        if val <= 0:
            return
        eng.wait_ge(self.sem[s], 16 * val)


GCH = {}
DCH = {}
for _call in (0, 1):
    _gj = 0
    _dj = 0
    for _ui in range(len(UNITS)):
        _mc = UNITS[_ui]["mc"]
        for _nt in range(NT):
            for _m in range(_mc):
                GCH[(_call, _ui, _nt, _m)] = _gj
                _gj += 1
        for _nt in range(NT):
            for _hc in range(HC):
                DCH[(_call, _ui, _nt, _hc)] = _dj
                _dj += 1


def build_nc():
    nc = bass.Bass("TRN2", target_bir_lowering=False)

    def inpt(name, shape, dtype=F32):
        return nc.dram_tensor(name, shape, dtype, kind="ExternalInput").ap()

    x_in = inpt("x_fm", [D, T])
    wq = inpt("wq", [D, D]); wk = inpt("wk", [D, D])
    wv = inpt("wv", [D, D]); wo = inpt("wo", [D, D])
    qb_i = inpt("qb", [D]); kb_i = inpt("kb", [D]); vb_i = inpt("vb", [D]); ob_i = inpt("ob", [D])
    gate_w = inpt("gate_w", [E, D]); gate_b = inpt("gate_b", [E])
    Wg = inpt("Wg", [66, 768, D]); Wu = inpt("Wu", [66, 768, D]); Wd = inpt("Wd", [66, D, 768])
    cos_c = inpt("cos_t", [128, T], BF16)
    sin_c = inpt("sin_t", [128, T], BF16)
    perm_c = inpt("perm", [128, 128], BF16)
    mask_c = inpt("masks", [128, 4 * 512], BF16)
    iden_c = inpt("iden", [128, 128], F32)
    y_out = nc.dram_tensor("y_fm", [D, T], F32, kind="ExternalOutput").ap()
    if DEV_STAGE:
        xn_out = nc.dram_tensor("xn_out", [D, T], BF16, kind="ExternalOutput").ap()
        rw_out = nc.dram_tensor("rw_out", [E, T], F32, kind="ExternalOutput").ap()
        dn_out = nc.dram_tensor("dn_out", [128, 16], F32, kind="ExternalOutput").ap()
        rcp_out = nc.dram_tensor("rcp_out", [128, 16], F32, kind="ExternalOutput").ap()
        sco_out = nc.dram_tensor("sco_out", [128, 1024], F32, kind="ExternalOutput").ap()
        sel_out = nc.dram_tensor("sel_out", [128, 1024], F32, kind="ExternalOutput").ap()

    gbf, ubf, dbf = [], [], []
    for ui, u in enumerate(UNITS):
        gbf.append(nc.dram_tensor(f"gbf{ui}", [HC, 768, 128], BF16))
        ubf.append(nc.dram_tensor(f"ubf{ui}", [HC, 768, 128], BF16))
        dbf.append(nc.dram_tensor(f"dbf{ui}", [6, D, 128], BF16))
    gwbf = nc.dram_tensor("gwbf", [E, D], BF16)
    rwT_d = nc.dram_tensor("rwT_d", [E, T], F32)

    cur = [nc.sbuf_base]

    def alloc(name, shape, dtype, align=32):
        sz = shape[-1] * mybir.dt.size(dtype)
        cur[0] = (cur[0] + align - 1) // align * align
        t = nc.alloc_sbuf_tensor_at(name, shape, dtype, offset=cur[0])
        cur[0] += sz
        return t

    x_fm = [alloc(f"x{c}", [128, T], F32) for c in range(HC)]
    xn = [alloc(f"xn{c}", [128, T], BF16) for c in range(HC)]
    iden = alloc("iden_sb", [128, 128], F32)
    perm = alloc("perm_sb", [128, 128], BF16)
    gate_wT = alloc("gate_wT", [128, 8 * 64], BF16)
    gb_bc = alloc("gb_bc", [128, E], F32)
    qb_pp = alloc("qb_pp", [128, 8], F32)
    kb_pp = alloc("kb_pp", [128, 8], F32)
    ob_pp = alloc("ob_pp", [128, 8], F32)
    ones_col = alloc("ones_col", [128, 1], F32)
    ones_colb = alloc("ones_colb", [128, 1], BF16)
    pdum = alloc("pdum", [1, 1], F32)
    ones_row = alloc("ones_row", [1, 128], F32)
    ones_rowb = alloc("ones_rowb", [1, 128], BF16)
    persist_end = cur[0]

    cur[0] = persist_end
    wgT2 = [alloc(f"wgTs{s_}", [128, HC * 768], BF16) for s_ in range(2)]
    wuT2 = [alloc(f"wuTs{s_}", [128, HC * 768], BF16) for s_ in range(2)]
    wdT2 = [alloc(f"wdTs{s_}", [128, 6 * 1024], BF16) for s_ in range(2)]
    rw_bc2 = [alloc(f"rw_bcs{s_}", [128, T], BF16) for s_ in range(2)]
    hact = [alloc(f"hact{c}", [128, 6 * 512], BF16) for c in range(2)]
    sg_st = [alloc(f"sg{i}", [128, 512], F32) for i in range(2)]
    sgu_st = [alloc(f"sgu{i}", [128, 512], F32) for i in range(2)]
    ffn_end = cur[0]

    cur[0] = persist_end
    qh = [alloc(f"qh{c}", [128, T], BF16) for c in range(4)]
    kh = [alloc(f"kh{c}", [128, T], BF16) for c in range(4)]
    v_tm = alloc("v_tm", [128, TC * 520], BF16)
    ctxh = [alloc(f"ctxh{c}", [128, T], BF16) for c in range(4)]
    whalf = alloc("whalf", [128, 4096], BF16)
    attnT = [alloc(f"attnT{i}", [128, 512], BF16) for i in range(2)]
    rope_t0 = attnT[0]
    rope_t1 = attnT[1]
    cos_sb = alloc("cos_sb", [128, T], BF16)
    sin_sb = alloc("sin_sb", [128, T], BF16)
    mask_sb = alloc("mask_sb", [128, 4 * 512], BF16)
    vb_row = alloc("vb_row", [1, D], BF16)
    v_st = alloc("v_st", [128, 512], F32)
    sqst = v_st
    nmax = alloc("nmax", [1, 16], F32)
    nb_sb = [alloc(f"nb{c}", [128, 1], F32) for c in range(4)]
    rs_sb = alloc("rs_sb", [1, 512], F32)
    rs_bf = vb_row
    ctx_st = v_st
    attn_end = cur[0]

    cur[0] = persist_end
    r_scores = alloc("r_scores", [128, 1024], F32)
    r_sfc = alloc("r_sfc", [128, 1024], F32)
    r_a = alloc("r_a", [128, 1024], F32)
    r_b = alloc("r_b", [128, 1024], F32)
    r_m1 = alloc("r_m1", [128, 128], F32)
    r_m2 = alloc("r_m2", [128, 128], F32)
    r_gs = alloc("r_gs", [128, 128], F32)
    r_gw = alloc("r_gw", [128, 128], F32)
    r_gm = alloc("r_gm", [128, 128], F32)
    r_mj = alloc("r_mj", [128, 16], F32)
    r_c = alloc("r_c", [128, 1024], F32)
    r_dn = alloc("r_dn", [128, 16], F32)
    rwT_sb = alloc("rwT_sb", [64, T], F32)
    ln_a = alloc("ln_a", [1, T], F32)
    ln_b = alloc("ln_b", [1, T], F32)
    ln_c = alloc("ln_c", [1, T], F32)
    ln_st = [alloc(f"lnst{i}", [128, 512], F32) for i in range(2)]
    xsq_st = [alloc(f"xsq{i}", [128, 512], F32) for i in range(8)]
    rout_end = cur[0]
    assert rout_end <= attn_end, (rout_end, attn_end)
    assert max(ffn_end, attn_end) <= nc.sbuf_top, (ffn_end, attn_end, nc.sbuf_top)

    ps = [nc.psum_tensor(f"ps{i}", [128, 512], F32).__enter__() for i in range(8)]

    p = P(nc)

    def emit_unit_casts(g, ui):
        sg_ = _ap(Wg.tensor, ui * 768 * D, [[128, HC], [D, 768], [1, 128]])
        su_ = _ap(Wu.tensor, ui * 768 * D, [[128, HC], [D, 768], [1, 128]])
        p.dinc(g.dma_start(_ap(gbf[ui], 0, [[768 * 128, HC], [128, 768], [1, 128]]), sg_), "cast")
        p.dinc(g.dma_start(_ap(ubf[ui], 0, [[768 * 128, HC], [128, 768], [1, 128]]), su_), "cast")
        sd_ = _ap(Wd.tensor, ui * D * 768, [[768, D], [128, 6], [1, 128]])
        p.dinc(g.dma_start(_ap(dbf[ui], 0, [[128, D], [D * 128, 6], [1, 128]]), sd_),
               "cast", label=f"cast_u{ui}")

    def prologue():
        with p.block() as blk:
            def sync_fn(sy):
                for c in range(HC):
                    p.dinc(sy.dma_start(x_fm[c][:], x_in[c * 128:(c + 1) * 128, :]), "wload")
                p.dinc(sy.dma_start(iden[:], iden_c), "wload")
                p.dinc(sy.dma_start(perm[:], perm_c), "wload")
                p.dinc(sy.dma_start(gb_bc[:], _ap(gate_b.tensor, 0, [[0, 128], [1, E]])), "wload")
                for oc in range(8):
                    p.dinc(sy.dma_start(qb_pp[:, oc:oc + 1], _ap(qb_i.tensor, oc * 128, [[1, 128], [1, 1]])), "wload")
                for oc in range(8):
                    p.dinc(sy.dma_start(kb_pp[:, oc:oc + 1], _ap(kb_i.tensor, oc * 128, [[1, 128], [1, 1]])), "wload")
                for oc in range(8):
                    p.dinc(sy.dma_start(ob_pp[:, oc:oc + 1], _ap(ob_i.tensor, oc * 128, [[1, 128], [1, 1]])), "wload")
                p.mark("prologue_loads", "wload")
            p.on(blk, "sync", sync_fn)

            def g_fn(g):
                g.memset(ones_col[:], 1.0)
                g.memset(ones_colb[:], 1.0)
                g.memset(ones_row[:], 1.0)
                i = g.memset(ones_rowb[:], 1.0)
                p.inc(i, "pool", label="ones_ready")
                if (DO_FFN1 or DO_FFN2) and DEV_STAGE != "ln":
                    p.dinc(g.dma_start(gwbf.ap(), gate_w), "cast", label="cast_gw")
                    if DEV_STAGE != "rt":
                        for ui in range(min(3, N_UNITS)):
                            emit_unit_casts(g, ui)
                    p.dwait(g, "cast", val=p.cnt["cast"])
                    i = g.memset(pdum[:], 0.0)
                    p.inc(i, "pool", label="cast_pro")
            p.on(blk, "gpsimd", g_fn)

    def ln_stage(call):
        tag = f"ln{call}"
        with p.block() as blk:
            def t_fn(t):
                p.dwait(t, "wload", label="prologue_loads")
                for nt in range(NT):
                    for hc in range(HC):
                        i = t.matmul(ps[6][0:1, :], ones_col[:], x_fm[hc][:, nt * 512:(nt + 1) * 512],
                                     start=(hc == 0), stop=(hc == HC - 1))
                    p.inc(i, "pe")
                    for hc in range(HC):
                        p.wait(t, "act", label=f"{tag}_sq{nt}_{hc}")
                        i = t.matmul(ps[7][0:1, :], ones_col[:], xsq_st[hc][:],
                                     start=(hc == 0), stop=(hc == HC - 1))
                    p.inc(i, "pe", label=f"{tag}_stats{nt}")
                    p.wait(t, "dve", label=f"{tag}_evac{nt}")
                p.wait(t, "dve", label=f"{tag}_rows")
                for nt in range(NT):
                    i = t.matmul(ps[6][:, :], ones_row[:], ln_c[0:1, nt * 512:(nt + 1) * 512],
                                 start=True, stop=True)
                    p.inc(i, "pe")
                    i = t.matmul(ps[7][:, :], ones_row[:], ln_a[0:1, nt * 512:(nt + 1) * 512],
                                 start=True, stop=True)
                    p.inc(i, "pe", label=f"{tag}_bc{nt}")
                    p.wait(t, "dve", label=f"{tag}_norm{nt}")

            def s_fn(s):
                p.dwait(s, "wload", label="prologue_loads")
                for nt in range(NT):
                    for hc in range(HC):
                        if nt > 0:
                            p.wait(s, "pe", label=f"{tag}_stats{nt - 1}")
                        i = s.square(xsq_st[hc][:], x_fm[hc][:, nt * 512:(nt + 1) * 512])
                        p.inc(i, "act", label=f"{tag}_sq{nt}_{hc}")
                p.wait(s, "dve", label=f"{tag}_var")
                p.chain(s, "act", s.activation(ln_c[:], ln_b[:], AF.Ln))
                i = s.activation(ln_c[:], ln_c[:], AF.Exp, scale=-0.5)
                p.inc(i, "act", label=f"{tag}_sd")

            def v_fn(v):
                for nt in range(NT):
                    p.wait(v, "pe", label=f"{tag}_stats{nt}")
                    p.chain(v, "dve", v.tensor_copy(ln_a[0:1, nt * 512:(nt + 1) * 512], ps[6][0:1, :]))
                    i = v.tensor_copy(ln_b[0:1, nt * 512:(nt + 1) * 512], ps[7][0:1, :])
                    p.inc(i, "dve", label=f"{tag}_evac{nt}")
                p.wait(v, "dve", val=p.cnt["dve"])
                p.chain(v, "dve", v.tensor_scalar(ln_a[:], ln_a[:], 1.0 / D, 0.0, ALU.mult, ALU.add))
                p.chain(v, "dve", v.tensor_scalar(ln_b[:], ln_b[:], 1.0 / D, 0.0, ALU.mult, ALU.add))
                p.chain(v, "dve", v.tensor_tensor(ln_c[:], ln_a[:], ln_a[:], ALU.mult))
                p.chain(v, "dve", v.tensor_tensor(ln_b[:], ln_b[:], ln_c[:], ALU.subtract))
                i = v.tensor_scalar(ln_b[:], ln_b[:], LN_EPS, 0.0, ALU.add, ALU.add)
                p.inc(i, "dve", label=f"{tag}_var")
                p.wait(v, "act", label=f"{tag}_sd")
                i = v.tensor_tensor(ln_a[:], ln_a[:], ln_c[:], ALU.mult)
                p.inc(i, "dve", label=f"{tag}_rows")
                p.wait(v, "dve", val=p.cnt["dve"])
                for nt in range(NT):
                    p.wait(v, "pe", label=f"{tag}_bc{nt}")
                    for hc in range(HC):
                        p.chain(v, "dve", v.tensor_tensor(ln_st[hc % 2][:],
                                                          x_fm[hc][:, nt * 512:(nt + 1) * 512],
                                                          ps[6][:], ALU.mult))
                        i = v.tensor_tensor(xn[hc][:, nt * 512:(nt + 1) * 512],
                                            ln_st[hc % 2][:], ps[7][:], ALU.subtract)
                    p.inc(i, "dve", label=f"{tag}_norm{nt}")

            p.on(blk, "tensor", t_fn)
            p.on(blk, "scalar", s_fn)
            p.on(blk, "vector", v_fn)

    def router_stage(call):
        tag = f"rt{call}"

        def sc(t_, dims, off=0):
            return _ap(t_, off, dims)

        with p.block() as blk:
            def t_fn(t):
                if call == 0:
                    p.dwait(t, "wload", label=f"{tag}_gwT")
                for tc in range(TC):
                    if tc >= 2:
                        p.wait(t, "act", label=f"{tag}_sig{tc - 2}")
                    for kc in range(HC):
                        i = t.matmul(ps[6 + tc % 2][0:128, 0:64], xn[kc][:, tc * 128:(tc + 1) * 128],
                                     gate_wT[:, kc * 64:(kc + 1) * 64],
                                     start=(kc == 0), stop=(kc == HC - 1))
                    p.inc(i, "pe", label=f"{tag}_lg{tc}")
                p.wait(t, "dve", label=f"{tag}_rw")
                for tc in range(TC):
                    if tc >= 2:
                        p.wait(t, "dve", label=f"{tag}_tev{tc - 2}")
                    i = t.transpose(ps[6 + tc % 2][0:64, 0:128], r_sfc[:, tc * 64:(tc + 1) * 64], iden[:])
                    p.inc(i, "pe", label=f"{tag}_tr{tc}")

            def s_fn(s):
                for tc in range(TC):
                    p.wait(s, "pe", label=f"{tag}_lg{tc}")
                    i = s.activation(r_scores[:, tc * 64:(tc + 1) * 64], ps[6 + tc % 2][0:128, 0:64],
                                     AF.Sigmoid)
                    p.inc(i, "act", label=f"{tag}_sig{tc}")
                if True:
                    p.wait(s, "dve", label=f"{tag}_dn")
                    p.chain(s, "act", s.activation(r_mj[:], r_dn[:], AF.Ln))
                    i = s.activation(r_mj[:], r_mj[:], AF.Exp, scale=-1.0)
                    p.inc(i, "act", label=f"{tag}_rcp")

            def v_fn(v):
                p.wait(v, "act", label=f"{tag}_sig{TC - 1}")
                p.chain(v, "dve", v.tensor_tensor(sc(r_sfc, [[1024, 128], [64, 16], [1, 64]]), sc(r_scores, [[1024, 128], [64, 16], [1, 64]]), sc(gb_bc, [[64, 128], [0, 16], [1, 64]]), ALU.add))
                p.chain(v, "dve", v.tensor_reduce(r_m1[:], sc(r_sfc, [[1024, 128], [8, 128], [1, 8]]), axis=mybir.AxisListType.X, op=ALU.max))
                p.chain(v, "dve", v.tensor_tensor(sc(r_a, [[1024, 128], [8, 128], [1, 8]]), sc(r_sfc, [[1024, 128], [8, 128], [1, 8]]), sc(r_m1, [[128, 128], [1, 128], [0, 8]]), ALU.is_equal))
                p.chain(v, "dve", v.scalar_tensor_tensor(r_b[:], r_a[:], -BIG, r_sfc[:], ALU.mult, ALU.add))
                p.chain(v, "dve", v.tensor_reduce(r_m2[:], sc(r_b, [[1024, 128], [8, 128], [1, 8]]), axis=mybir.AxisListType.X, op=ALU.max))
                p.chain(v, "dve", v.tensor_tensor(r_gs[:], r_m1[:], r_m2[:], ALU.add))
                p.chain(v, "dve", v.tensor_copy(r_gw[:], r_gs[:]))
                p.chain(v, "dve", v.memset(r_gm[:], 0.0))
                for j in range(4):
                    p.chain(v, "dve", v.tensor_reduce(r_mj[:], sc(r_gw, [[128, 128], [8, 16], [1, 8]]), axis=mybir.AxisListType.X, op=ALU.max))
                    p.chain(v, "dve", v.tensor_tensor(sc(r_m1, [[128, 128], [8, 16], [1, 8]]), sc(r_gw, [[128, 128], [8, 16], [1, 8]]), sc(r_mj, [[16, 128], [1, 16], [0, 8]]), ALU.is_equal))
                    p.chain(v, "dve", v.tensor_tensor(r_gm[:], r_gm[:], r_m1[:], ALU.add))
                    p.chain(v, "dve", v.scalar_tensor_tensor(r_gw[:], r_m1[:], -BIG, r_gw[:], ALU.mult, ALU.add))
                p.chain(v, "dve", v.tensor_tensor(sc(r_b, [[1024, 128], [64, 16], [8, 8], [1, 8]]), sc(r_sfc, [[1024, 128], [64, 16], [8, 8], [1, 8]]), sc(r_gm, [[128, 128], [8, 16], [1, 8], [0, 8]]), ALU.mult))
                p.chain(v, "dve", v.memset(r_a[:], 0.0))
                p.chain(v, "dve", v.memset(r_dn[:], 0.0))
                for j in range(8):
                    p.chain(v, "dve", v.tensor_reduce(r_mj[:], sc(r_b, [[1024, 128], [64, 16], [1, 64]]), axis=mybir.AxisListType.X, op=ALU.max))
                    p.chain(v, "dve", v.tensor_tensor(sc(r_sfc, [[1024, 128], [64, 16], [1, 64]]), sc(r_b, [[1024, 128], [64, 16], [1, 64]]), sc(r_mj, [[16, 128], [1, 16], [0, 64]]), ALU.is_equal))
                    p.chain(v, "dve", v.tensor_tensor(r_a[:], r_a[:], r_sfc[:], ALU.add))
                    p.chain(v, "dve", v.tensor_tensor(r_c[:], r_sfc[:], r_scores[:], ALU.mult))
                    p.chain(v, "dve", v.tensor_reduce(r_mj[:], sc(r_c, [[1024, 128], [64, 16], [1, 64]]), axis=mybir.AxisListType.X, op=ALU.max))
                    p.chain(v, "dve", v.tensor_tensor(r_dn[:], r_dn[:], r_mj[:], ALU.add))
                    p.chain(v, "dve", v.scalar_tensor_tensor(r_b[:], r_sfc[:], -BIG, r_b[:], ALU.mult, ALU.add))
                p.chain(v, "dve", v.tensor_tensor(r_a[:], r_a[:], r_scores[:], ALU.mult))
                i = v.tensor_scalar(r_dn[:], r_dn[:], 1.0e-20, 0.0, ALU.add, ALU.add)
                p.inc(i, "dve", label=f"{tag}_dn")
                p.wait(v, "act", label=f"{tag}_rcp")
                p.chain(v, "dve", v.tensor_tensor(sc(r_sfc, [[1024, 128], [64, 16], [1, 64]]), sc(r_a, [[1024, 128], [64, 16], [1, 64]]), sc(r_mj, [[16, 128], [1, 16], [0, 64]]), ALU.mult))
                i = v.tensor_scalar(r_sfc[:], r_sfc[:], RSCALE, 0.0, ALU.mult, ALU.add)
                p.inc(i, "dve", label=f"{tag}_rw")
                for tc in range(TC):
                    p.wait(v, "pe", label=f"{tag}_tr{tc}")
                    i = v.tensor_copy(rwT_sb[0:64, tc * 128:(tc + 1) * 128],
                                      ps[6 + tc % 2][0:64, 0:128])
                    p.inc(i, "dve", label=f"{tag}_tev{tc}")

            def sy_fn(sy):
                if call == 0:
                    p.wait(sy, "pool", label="cast_pro")
                    for kc in range(HC):
                        i = sy.dma_start_transpose(gate_wT[:, kc * 64:(kc + 1) * 64],
                                                   _ap(gwbf, kc * 128, [[D, E], [1, 128]]))
                        p.dinc(i, "wload")
                    p.mark(f"{tag}_gwT", "wload")
                p.wait(sy, "dve", label=f"{tag}_tev{TC - 1}")
                i = sy.dma_start(rwT_d.ap(), rwT_sb[:])
                p.dinc(i, "store", label=f"{tag}_rwT")
                p.dwait(sy, "store", val=p.cnt["store"])

            p.on(blk, "tensor", t_fn)
            p.on(blk, "scalar", s_fn)
            p.on(blk, "vector", v_fn)
            p.on(blk, "sync", sy_fn)

    def ffn_units(call):
        tag = f"f{call}"
        nu = min(N_UNITS, len(UNITS))
        order = [("gu", 0), ("gu", 1), ("d", 0), ("gu", 2), ("d", 1), ("gu", 3), ("d", 2), ("d", 3)]

        with p.block() as blk:
            def sy_fn(sy):
                for ui in range(nu):
                    u = UNITS[ui]
                    iu, mc = u["i"], u["mc"]
                    if ui <= 2:
                        p.wait(sy, "pool", label="cast_pro")
                    else:
                        p.wait(sy, "pool", label=f"castm{ui - 3}")
                    if ui > 1:
                        p.wait(sy, "pe", label=f"{tag}_gdone{ui - 2}")
                    elif call == 1:
                        p.wait(sy, "pe", label=f"f0_gdone{nu - 2 + ui}")
                    i = sy.dma_start_transpose(wgT2[ui % 2][:], _ap(gbf[ui], 0, [[128, HC * 768], [1, 128]]))
                    p.dinc(i, f"wg{ui % 2}")
                    p.mark(f"{tag}_gld{ui}", f"wg{ui % 2}")
                    if ui > 1:
                        p.wait(sy, "pe", label=f"{tag}_udone{ui - 2}")
                    elif call == 1:
                        p.wait(sy, "pe", label=f"f0_udone{nu - 2 + ui}")
                    i = sy.dma_start_transpose(wuT2[ui % 2][:], _ap(ubf[ui], 0, [[128, HC * 768], [1, 128]]))
                    p.dinc(i, f"wu{ui % 2}")
                    p.mark(f"{tag}_uld{ui}", f"wu{ui % 2}")
                    if ui > 1:
                        p.wait(sy, "pe", label=f"{tag}_ddone{ui - 2}")
                    elif call == 1:
                        p.wait(sy, "pe", label=f"f0_ddone{nu - 2 + ui}")
                    i = sy.dma_start_transpose(wdT2[ui % 2][:], _ap(dbf[ui], 0, [[128, 6 * D], [1, 128]]))
                    p.dinc(i, f"wd{ui % 2}")
                    p.mark(f"{tag}_dld{ui}", f"wd{ui % 2}")
                if DEV_UPART == "load":
                    p.dwait(sy, "wload")

            def t_fn(t):
                if DEV_UPART == "load":
                    return
                for ui in range(nu):
                    u = UNITS[ui]
                    mc, lc = u["mc"], u["lc"]
                    for kind, nt in order:
                        if kind == "d" and DEV_UPART == "gu":
                            continue
                        if kind == "gu":
                            for m in range(mc):
                                mw = lc if m == mc - 1 else 128
                                j = GCH[(call, ui, nt, m)]
                                if nt == 0 and m == 0:
                                    p.dwait(t, f"wg{ui % 2}", label=f"{tag}_gld{ui}")
                                if j >= 2:
                                    p.wait(t, "act", label=f"{tag}_silu{j - 2}")
                                for kc in range(HC):
                                    i = t.matmul(ps[j % 2][0:mw, :],
                                                 wgT2[ui % 2][:, kc * 768 + m * 128: kc * 768 + m * 128 + mw],
                                                 xn[kc][:, nt * 512:(nt + 1) * 512],
                                                 start=(kc == 0), stop=(kc == HC - 1))
                                p.inc(i, "pe", label=f"{tag}_g{j}")
                                if nt == 0 and m == 0:
                                    p.dwait(t, f"wu{ui % 2}", label=f"{tag}_uld{ui}")
                                if j >= 2:
                                    p.wait(t, "dve", label=f"{tag}_mult{j - 2}")
                                for kc in range(HC):
                                    i = t.matmul(ps[2 + j % 2][0:mw, :],
                                                 wuT2[ui % 2][:, kc * 768 + m * 128: kc * 768 + m * 128 + mw],
                                                 xn[kc][:, nt * 512:(nt + 1) * 512],
                                                 start=(kc == 0), stop=(kc == HC - 1))
                                p.inc(i, "pe", label=f"{tag}_u{j}")
                            if nt == 3:
                                jl = GCH[(call, ui, 3, mc - 1)]
                                if p.dry:
                                    p.marks[f"{tag}_gdone{ui}"] = p.marks[f"{tag}_g{jl}"]
                                    p.marks[f"{tag}_udone{ui}"] = p.marks[f"{tag}_u{jl}"]
                        else:
                            if nt == 0:
                                p.dwait(t, f"wd{ui % 2}", label=f"{tag}_dld{ui}")
                            if u["kind"] == "e":
                                p.wait(t, "pool", label=f"{tag}_hact{ui}_{nt}")
                            else:
                                p.wait(t, "dve", label=f"{tag}_hact{ui}_{nt}")
                            for hc in range(HC):
                                dj = DCH[(call, ui, nt, hc)]
                                if dj >= 2:
                                    p.wait(t, "dve", label=f"{tag}_dev{dj - 2}")
                                for ic in range(mc):
                                    kw = lc if ic == mc - 1 else 128
                                    i = t.matmul(ps[4 + dj % 2][:, :],
                                                 wdT2[ui % 2][0:kw, ic * 1024 + hc * 128: ic * 1024 + hc * 128 + 128],
                                                 hact[nt % 2][0:kw, ic * 512:(ic + 1) * 512],
                                                 start=(ic == 0), stop=(ic == mc - 1))
                                p.inc(i, "pe", label=f"{tag}_d{dj}")
                            p.mark(f"{tag}_dcol{ui}_{nt}", "pe")
                            if nt == 3:
                                p.mark(f"{tag}_ddone{ui}", "pe")

            def s_fn(s):
                if DEV_UPART == "load":
                    return
                for ui in range(nu):
                    u = UNITS[ui]
                    mc, lc = u["mc"], u["lc"]
                    for nt in range(NT):
                        for m in range(mc):
                            mw = lc if m == mc - 1 else 128
                            j = GCH[(call, ui, nt, m)]
                            p.wait(s, "pe", label=f"{tag}_g{j}")
                            if j >= 2:
                                p.wait(s, "dve", label=f"{tag}_mult{j - 2}")
                            i = s.activation(sg_st[j % 2][0:mw, :], ps[j % 2][0:mw, :], AF.Silu)
                            p.inc(i, "act", label=f"{tag}_silu{j}")

            def v_fn(v):
                if DEV_UPART == "load":
                    return
                for ui in range(nu):
                    u = UNITS[ui]
                    mc, lc = u["mc"], u["lc"]
                    routed = u["kind"] == "e"
                    for kind, nt in order:
                        if kind == "d" and DEV_UPART == "gu":
                            continue
                        if kind == "gu":
                            for m in range(mc):
                                mw = lc if m == mc - 1 else 128
                                j = GCH[(call, ui, nt, m)]
                                p.wait(v, "pe", label=f"{tag}_u{j}")
                                p.wait(v, "act", label=f"{tag}_silu{j}")
                                if routed:
                                    if j >= 2:
                                        p.wait(v, "pool", label=f"{tag}_rwm{j - 2}")
                                    i = v.tensor_tensor(sgu_st[j % 2][0:mw, :], sg_st[j % 2][0:mw, :],
                                                        ps[2 + j % 2][0:mw, :], ALU.mult)
                                    p.inc(i, "dve", label=f"{tag}_mult{j}")
                                else:
                                    gidx = ui * 4 + nt
                                    if gidx >= 2:
                                        ui2, nt2 = (gidx - 2) // 4, (gidx - 2) % 4
                                        p.wait(v, "pe", label=f"{tag}_dcol{ui2}_{nt2}")
                                    i = v.tensor_tensor(hact[nt % 2][0:mw, m * 512:(m + 1) * 512],
                                                        sg_st[j % 2][0:mw, :],
                                                        ps[2 + j % 2][0:mw, :], ALU.mult)
                                    c = p.inc(i, "dve", label=f"{tag}_mult{j}")
                                    if p.dry and m == mc - 1:
                                        p.marks[f"{tag}_hact{ui}_{nt}"] = c
                        else:
                            for hc in range(HC):
                                dj = DCH[(call, ui, nt, hc)]
                                p.wait(v, "pe", label=f"{tag}_d{dj}")
                                i = v.scalar_tensor_tensor(x_fm[hc][:, nt * 512:(nt + 1) * 512],
                                                           ps[4 + dj % 2][:, :], 0.5,
                                                           x_fm[hc][:, nt * 512:(nt + 1) * 512],
                                                           ALU.mult, ALU.add)
                                p.inc(i, "dve", label=f"{tag}_dev{dj}")

            def g_fn(g):
                if DEV_UPART == "load":
                    return
                p.dwait(g, "store", label=f"rt{call}_rwT")
                for e_ in range(min(2, nu)):
                    i = g.dma_start(rw_bc2[e_ % 2][:], _ap(rwT_d, e_ * T, [[0, 128], [1, T]]))
                    p.dinc(i, f"wrw{e_ % 2}")
                    p.mark(f"{tag}_rwld{e_}", f"wrw{e_ % 2}")
                for ui in range(nu):
                    u = UNITS[ui]
                    casted = False
                    if call == 0 and ui + 3 < nu:
                        emit_unit_casts(g, ui + 3)
                        casted = True
                    if u["kind"] != "e":
                        if casted:
                            p.dwait(g, "cast", val=p.cnt["cast"])
                            i = g.memset(pdum[:], 0.0)
                            p.inc(i, "pool", label=f"castm{ui}")
                        continue
                    mc, lc = u["mc"], u["lc"]
                    e_ = u["e"]
                    p.dwait(g, f"wrw{e_ % 2}", label=f"{tag}_rwld{ui}")
                    for nt in range(NT):
                        gidx = ui * 4 + nt
                        if gidx >= 2:
                            ui2, nt2 = (gidx - 2) // 4, (gidx - 2) % 4
                            p.wait(g, "pe", label=f"{tag}_dcol{ui2}_{nt2}")
                        for m in range(mc):
                            mw = lc if m == mc - 1 else 128
                            j = GCH[(call, ui, nt, m)]
                            p.wait(g, "dve", label=f"{tag}_mult{j}")
                            i = g.tensor_tensor(hact[nt % 2][0:mw, m * 512:(m + 1) * 512],
                                                sgu_st[j % 2][0:mw, :],
                                                rw_bc2[u["e"] % 2][0:mw, nt * 512:(nt + 1) * 512], ALU.mult)
                            c = p.inc(i, "pool", label=f"{tag}_rwm{j}")
                            if p.dry and m == mc - 1:
                                p.marks[f"{tag}_hact{ui}_{nt}"] = c
                    if e_ + 2 < E and e_ + 2 < nu:
                        p.wait(g, "pool", val=p.cnt["pool"])
                        i = g.dma_start(rw_bc2[e_ % 2][:], _ap(rwT_d, (e_ + 2) * T, [[0, 128], [1, T]]))
                        p.dinc(i, f"wrw{e_ % 2}")
                        p.mark(f"{tag}_rwld{e_ + 2}", f"wrw{e_ % 2}")
                    if casted:
                        p.dwait(g, "cast", val=p.cnt["cast"])
                        i = g.memset(pdum[:], 0.0)
                        p.inc(i, "pool", label=f"castm{ui}")
                    p.mark(f"{tag}_pl{ui}", "pool")

            p.on(blk, "sync", sy_fn)
            p.on(blk, "tensor", t_fn)
            p.on(blk, "scalar", s_fn)
            p.on(blk, "vector", v_fn)
            p.on(blk, "gpsimd", g_fn)

    def attn_consts():
        with p.block() as blk:
            def sy_fn(sy):
                p.dinc(sy.dma_start(cos_sb[:], cos_c), "wload")
                p.dinc(sy.dma_start(sin_sb[:], sin_c), "wload")
                p.dinc(sy.dma_start(mask_sb[:], mask_c), "wload")
                p.mark("at_consts", "wload")
            p.on(blk, "sync", sy_fn)

            def g_fn(g):
                p.dinc(g.dma_start(vb_row[:], _ap(vb_i.tensor, 0, [[D, 1], [1, D]])), "cast",
                       label="at_vb")
                i = g.memset(v_tm[:], 1.0)
                p.inc(i, "pool", label="at_vones")
            p.on(blk, "gpsimd", g_fn)

    def attn_pass(hg):
        tag = "at"
        with p.block() as blk:
            def g_fn(g):
                for wi, wt in enumerate([wq, wk, wv, wo]):
                    if wi == 0:
                        if hg == 1:
                            p.wait(g, "pe", label=f"{tag}_wfree1_0")
                    else:
                        p.wait(g, "pe", label=f"{tag}_wfree{hg}_{wi}")
                    if wi < 3:
                        src = _ap(wt.tensor, 512 * hg, [[D, 128], [128 * D, HC], [1, 512]])
                        dst = _ap(whalf, 0, [[4096, 128], [512, HC], [1, 512]])
                    else:
                        src = _ap(wt.tensor, 512 * hg * D, [[D, 128], [128 * D, 4], [1, 1024]])
                        dst = _ap(whalf, 0, [[4096, 128], [1024, 4], [1, 1024]])
                    p.dinc(g.dma_start(dst, src), "cast", label=f"{tag}_w{hg}_{wi}")

            def t_fn(t):
                cc = 0
                for wi in range(3):
                    p.dwait(t, "cast", label=f"{tag}_w{hg}_{wi}")
                    if wi < 2:
                        for ocl in range(4):
                            for tn in range(NT):
                                if cc >= 2:
                                    p.wait(t, "dve", label=f"{tag}_pj{hg}_{cc - 2}")
                                for kc in range(HC):
                                    i = t.matmul(ps[4 + cc % 2][:, :],
                                                 _ap(whalf, kc * 512 + ocl * 128, [[4096, 128], [1, 128]]),
                                                 xn[kc][:, tn * 512:(tn + 1) * 512],
                                                 start=(kc == 0), stop=(kc == HC - 1))
                                p.inc(i, "pe", label=f"{tag}_pc{hg}_{cc}")
                                cc += 1
                    else:
                        p.wait(t, "pool", label="ones_ready")
                        p.dwait(t, "wload", label="at_consts")
                        for tc in range(TC):
                            if cc >= 34:
                                p.wait(t, "act", label=f"{tag}_vev{hg}_{cc - 2}")
                            elif cc >= 2:
                                p.wait(t, "dve", label=f"{tag}_pj{hg}_{cc - 2}")
                            for kc in range(HC):
                                t.matmul(ps[4 + cc % 2][:, :],
                                         xn[kc][:, tc * 128:(tc + 1) * 128],
                                         _ap(whalf, kc * 512, [[4096, 128], [1, 512]]),
                                         start=(kc == 0), stop=False)
                            i = t.matmul(ps[4 + cc % 2][:, :], ones_rowb[:],
                                         _ap(vb_row, 512 * hg, [[D, 1], [1, 512]]),
                                         start=False, stop=True)
                            p.inc(i, "pe", label=f"{tag}_pc{hg}_{cc}")
                            cc += 1
                    p.mark(f"{tag}_wfree{hg}_{wi + 1}", "pe")
                if DEV_APART == "qkv":
                    return
                rp = 0
                for qk, src in enumerate((qh, kh)):
                    for c in range(4):
                        for tn in range(NT):
                            p.wait(t, "dve", label=f"{tag}_pj{hg}_{(0 if qk == 0 else 16) + c * 4 + tn}")
                            if rp >= 2:
                                p.wait(t, "dve", label=f"{tag}_rt1_{hg}_{rp - 2}")
                            i = t.matmul(ps[6 + rp % 2][:, :], perm[:],
                                         src[c][:, tn * 512:(tn + 1) * 512], start=True, stop=True)
                            p.inc(i, "pe", label=f"{tag}_perm{hg}_{rp}")
                            rp += 1
                if DEV_APART == "rope":
                    return
                p.wait(t, "dve", label=f"{tag}_rt1_{hg}_{31}")
                p.wait(t, "dve", label=f"{tag}_rt1_{hg}_{30}")
                for c in range(4):
                    for tn in range(NT):
                        for qk in range(2):
                            p.wait(t, "act", label=f"{tag}_bsq{hg}_{c}_{tn}_{qk}")
                            i = t.matmul(ps[7][0:1, 0:512], ones_col[:], sqst[:],
                                         start=True, stop=True)
                            p.inc(i, "pe", label=f"{tag}_bmm{hg}_{c}_{tn}_{qk}")
                            p.wait(t, "dve", label=f"{tag}_bmx{hg}_{c}_{tn}_{qk}")
                p.wait(t, "dve", label=f"{tag}_nbv{hg}")
                for c in range(4):
                    i = t.matmul(ps[7][:, 0:1], ones_row[:], nmax[0:1, 8 + c:9 + c],
                                 start=True, stop=True)
                    p.inc(i, "pe", label=f"{tag}_nbb{hg}_{c}")
                    p.wait(t, "act", label=f"{tag}_nbc{hg}_{c}")
                if DEV_APART == "bound":
                    return
                sti = 0
                cgi = 0
                last_odd = [None]
                for hl in range(8):
                    c, rb = hl // 2, 64 * (hl % 2)
                    odd = hl % 2 == 1
                    for qb in range(NT):
                        kmax = 4 * qb + 4

                        def ctx_mm(kcp, last):
                            sip = sti + kcp
                            if kcp == 0 and cgi >= 1:
                                p.wait(t, "act", label=f"{tag}_cst{hg}_{cgi - 1}")
                            if odd and kcp == 0 and last_odd[0] is not None:
                                p.wait(t, "act", label=f"{tag}_rsx{hg}_{last_odd[0]}")
                            diag = kcp >= 4 * qb
                            p.wait(t, "dve" if diag else "act",
                                   label=(f"{tag}_msk{hg}_{sip}" if diag else f"{tag}_exp{hg}_{sip}"))
                            if not odd:
                                i = t.matmul(ps[2][0:65, :],
                                             _ap(v_tm, kcp * 520 + 65 * hl, [[520 * 16, 128], [1, 65]]),
                                             attnT[sip % 2][:], start=(kcp == 0), stop=last)
                                p.inc(i, "pe", label=f"{tag}_ctx{hg}_{sip}")
                            else:
                                i = t.matmul(ps[6][0:1, :], ones_colb[:, 0:1],
                                             attnT[sip % 2][:], start=(kcp == 0), stop=last)
                                if last:
                                    p.inc(i, "pe", label=f"{tag}_sum{hg}_{sip}")
                                i = t.matmul(ps[2][64:128, :],
                                             _ap(v_tm, kcp * 520 + 65 * hl, [[520 * 16, 128], [1, 64]]),
                                             attnT[sip % 2][:], start=(kcp == 0), stop=last,
                                             tile_position=(0, 64))
                                p.inc(i, "pe", label=f"{tag}_ctx{hg}_{sip}")

                        for kc in range(kmax):
                            si = sti + kc
                            if si >= 2:
                                p.wait(t, "act", label=f"{tag}_exp{hg}_{si - 2}")
                            i = t.matmul(ps[si % 2][:, :],
                                         kh[c][rb:rb + 64, kc * 128:(kc + 1) * 128],
                                         qh[c][rb:rb + 64, qb * 512:(qb + 1) * 512],
                                         start=True, stop=True)
                            p.inc(i, "pe", label=f"{tag}_sT{hg}_{si}")
                            if kc >= 1:
                                ctx_mm(kc - 1, kc - 1 == kmax - 1)
                        ctx_mm(kmax - 1, True)
                        sti += kmax
                        p.wait(t, "act", label=f"{tag}_rsx{hg}_{cgi}")
                        if cgi >= 1:
                            p.wait(t, "dve", label=f"{tag}_ctxev{hg}_{cgi - 1}")
                        i = t.matmul(ps[3][64:128, :] if odd else ps[3][0:64, :],
                                     ones_rowb[0:1, 0:64], rs_bf[0:1, 0:512], start=True, stop=True,
                                     tile_position=(0, 64) if odd else None)
                        p.inc(i, "pe", label=f"{tag}_rb{hg}_{cgi}")
                        if odd:
                            last_odd[0] = cgi
                        cgi += 1
                if DEV_APART in ("head", "h1", "h2", "h2a", "h2b"):
                    return
                p.dwait(t, "cast", label=f"{tag}_w{hg}_3")
                p.wait(t, "dve", label=f"{tag}_ctxev{hg}_{cgi - 1}")
                oj = 0
                for oc in range(8):
                    for tn in range(NT):
                        if oj >= 2:
                            p.wait(t, "dve", label=f"{tag}_oev{hg}_{oj - 2}")
                        for kcl in range(4):
                            i = t.matmul(ps[4 + oj % 2][:, :],
                                         _ap(whalf, kcl * 1024 + oc * 128, [[4096, 128], [1, 128]]),
                                         ctxh[kcl][:, tn * 512:(tn + 1) * 512],
                                         start=(kcl == 0), stop=(kcl == 3))
                        p.inc(i, "pe", label=f"{tag}_oc{hg}_{oj}")
                        oj += 1
                p.mark(f"{tag}_wfree{hg + 1}_0", "pe")

            def s_fn(s):
                for tc in range(TC):
                    cc = 32 + tc
                    p.wait(s, "pe", label=f"{tag}_pc{hg}_{cc}")
                    if tc >= 1:
                        p.wait(s, "dve", label=f"{tag}_pj{hg}_{cc - 1}")
                    i = s.copy(v_st[:], ps[4 + cc % 2][:, :])
                    p.inc(i, "act", label=f"{tag}_vev{hg}_{cc}")
                if DEV_APART == "qkv":
                    return
                if DEV_APART == "rope":
                    return
                first_sq = [True]
                for c in range(4):
                    for tn in range(NT):
                        for qk, src in enumerate((qh, kh)):
                            p.wait(s, "dve", label=f"{tag}_rfin{hg}_{qk}_{c}_{tn}")
                            p.wait(s, "pe", label=f"{tag}_bprev{hg}_{c}_{tn}_{qk}")
                            if first_sq[0]:
                                p.wait(s, "dve", label=f"{tag}_pj{hg}_47")
                                first_sq[0] = False
                            i = s.square(sqst[:], src[c][:, tn * 512:(tn + 1) * 512])
                            p.inc(i, "act", label=f"{tag}_bsq{hg}_{c}_{tn}_{qk}")
                p.wait(s, "dve", label=f"{tag}_nmaxv{hg}")
                i = s.activation(nmax[0:1, 0:8], nmax[0:1, 0:8], AF.Sqrt)
                p.inc(i, "act", label=f"{tag}_nsq{hg}")
                for c in range(4):
                    p.wait(s, "pe", label=f"{tag}_nbb{hg}_{c}")
                    i = s.copy(nb_sb[c][:], ps[7][:, 0:1])
                    p.inc(i, "act", label=f"{tag}_nbc{hg}_{c}")
                if DEV_APART == "bound":
                    return
                sti = 0
                cgi = 0
                for hl in range(8):
                    c = hl // 2
                    odd = hl % 2 == 1
                    for qb in range(NT):
                        kmax = 4 * qb + 4
                        for kc in range(kmax):
                            p.wait(s, "pe", label=f"{tag}_sT{hg}_{sti}")
                            if sti >= 2:
                                p.wait(s, "pe", label=f"{tag}_ctx{hg}_{sti - 2}")
                            i = s.activation(attnT[sti % 2][:], ps[sti % 2][:], AF.Exp,
                                             bias=nb_sb[c][:, 0:1], scale=0.125)
                            p.inc(i, "act", label=f"{tag}_exp{hg}_{sti}")
                            sti += 1
                        p.wait(s, "pe", label=f"{tag}_ctx{hg}_{sti - 1}")
                        if not odd:
                            c_ = p.inc(s.copy(ctx_st[0:65, :], ps[2][0:65, :]),
                                       "act", label=f"{tag}_cst{hg}_{cgi}")
                            if not p.dry:
                                s.wait_ge(p.sem["act"], c_)
                            p.chain(s, "act", s.activation(rs_sb[0:1, :], ctx_st[64:65, :], AF.Ln))
                        else:
                            p.wait(s, "pe", label=f"{tag}_sum{hg}_{sti - 1}")
                            c_ = p.inc(s.copy(ctx_st[64:128, :], ps[2][64:128, :]),
                                       "act", label=f"{tag}_cst{hg}_{cgi}")
                            if not p.dry:
                                s.wait_ge(p.sem["act"], c_)
                            p.chain(s, "act", s.activation(rs_sb[0:1, :], ps[6][0:1, :], AF.Ln))
                        i = s.activation(rs_bf[0:1, 0:512], rs_sb[0:1, :], AF.Exp, scale=-1.0)
                        p.inc(i, "act", label=f"{tag}_rsx{hg}_{cgi}")
                        cgi += 1

            def v_fn(v):
                cc = 0
                for wi in range(3):
                    if wi < 2:
                        dstl = qh if wi == 0 else kh
                        bias = qb_pp if wi == 0 else kb_pp
                        for ocl in range(4):
                            for tn in range(NT):
                                p.wait(v, "pe", label=f"{tag}_pc{hg}_{cc}")
                                i = v.tensor_scalar(dstl[ocl][:, tn * 512:(tn + 1) * 512],
                                                    ps[4 + cc % 2][:, :],
                                                    bias[:, 4 * hg + ocl:4 * hg + ocl + 1], 0.0,
                                                    ALU.add, ALU.add)
                                p.inc(i, "dve", label=f"{tag}_pj{hg}_{cc}")
                                cc += 1
                    else:
                        p.wait(v, "pool", label="at_vones")
                        for tc in range(TC):
                            p.wait(v, "act", label=f"{tag}_vev{hg}_{cc}")
                            i = v.tensor_copy(_ap(v_tm, tc * 520, [[520 * 16, 128], [65, 8], [1, 64]]),
                                              v_st[:])
                            p.inc(i, "dve", label=f"{tag}_pj{hg}_{cc}")
                            cc += 1
                if DEV_APART == "qkv":
                    return
                rp = 0
                for qk, src in enumerate((qh, kh)):
                    for c in range(4):
                        for tn in range(NT):
                            p.chain(v, "dve", v.tensor_tensor(rope_t0[:],
                                              src[c][:, tn * 512:(tn + 1) * 512],
                                              cos_sb[:, tn * 512:(tn + 1) * 512], ALU.mult))
                            p.wait(v, "pe", label=f"{tag}_perm{hg}_{rp}")
                            i = v.tensor_tensor(rope_t1[:], ps[6 + rp % 2][:, :],
                                                sin_sb[:, tn * 512:(tn + 1) * 512], ALU.mult)
                            c_ = p.inc(i, "dve", label=f"{tag}_rt1_{hg}_{rp}")
                            if not p.dry:
                                v.wait_ge(p.sem["dve"], c_)
                            i = v.tensor_tensor(src[c][:, tn * 512:(tn + 1) * 512],
                                                rope_t0[:], rope_t1[:], ALU.add)
                            p.inc(i, "dve", label=f"{tag}_rfin{hg}_{qk}_{c}_{tn}")
                            p.wait(v, "dve", val=p.cnt["dve"])
                            rp += 1
                if DEV_APART == "rope":
                    return
                for c in range(4):
                    for tn in range(NT):
                        for qk in range(2):
                            p.wait(v, "pe", label=f"{tag}_bmm{hg}_{c}_{tn}_{qk}")
                            col = qk * 4 + c
                            if tn == 0:
                                i = v.tensor_reduce(nmax[0:1, col:col + 1], ps[7][0:1, 0:512],
                                                    axis=mybir.AxisListType.X, op=ALU.max)
                            else:
                                p.chain(v, "dve", v.tensor_reduce(nmax[0:1, 15:16], ps[7][0:1, 0:512],
                                                axis=mybir.AxisListType.X, op=ALU.max))
                                i = v.tensor_tensor(nmax[0:1, col:col + 1], nmax[0:1, col:col + 1],
                                                    nmax[0:1, 15:16], ALU.max)
                            p.inc(i, "dve", label=f"{tag}_bmx{hg}_{c}_{tn}_{qk}")
                p.mark(f"{tag}_nmaxv{hg}", "dve")
                p.wait(v, "act", label=f"{tag}_nsq{hg}")
                p.chain(v, "dve", v.tensor_tensor(nmax[0:1, 8:12], nmax[0:1, 0:4], nmax[0:1, 4:8], ALU.mult))
                i = v.tensor_scalar(nmax[0:1, 8:12], nmax[0:1, 8:12], -0.125, 0.0, ALU.mult, ALU.add)
                p.inc(i, "dve", label=f"{tag}_nbv{hg}")
                if DEV_APART == "bound":
                    return
                sti = 0
                cgi = 0
                for hl in range(8):
                    c, rb = hl // 2, 64 * (hl % 2)
                    for qb in range(NT):
                        kmax = 4 * qb + 4
                        for kc in range(kmax):
                            if kc >= 4 * qb:
                                rel = kc - 4 * qb
                                p.wait(v, "act", label=f"{tag}_exp{hg}_{sti}")
                                i = v.tensor_tensor(attnT[sti % 2][:], attnT[sti % 2][:],
                                                    mask_sb[:, rel * 512:(rel + 1) * 512], ALU.mult)
                                p.inc(i, "dve", label=f"{tag}_msk{hg}_{sti}")
                            sti += 1
                        p.wait(v, "act", label=f"{tag}_cst{hg}_{cgi}")
                        p.wait(v, "pe", label=f"{tag}_rb{hg}_{cgi}")
                        if hl % 2 == 0:
                            i = v.tensor_tensor(ctxh[c][0:64, qb * 512:(qb + 1) * 512],
                                                ctx_st[0:64, :], ps[3][0:64, :], ALU.mult)
                        else:
                            i = v.tensor_tensor(ctxh[c][64:128, qb * 512:(qb + 1) * 512],
                                                ctx_st[64:128, :], ps[3][64:128, :], ALU.mult)
                        p.inc(i, "dve", label=f"{tag}_ctxev{hg}_{cgi}")
                        cgi += 1
                if DEV_APART in ("head", "h1", "h2", "h2a", "h2b"):
                    return
                for oj in range(32):
                    oc, tn = oj // 4, oj % 4
                    p.wait(v, "pe", label=f"{tag}_oc{hg}_{oj}")
                    if hg == 0:
                        i = v.scalar_tensor_tensor(x_fm[oc][:, tn * 512:(tn + 1) * 512],
                                                   ps[4 + oj % 2][:, :], ob_pp[:, oc:oc + 1],
                                                   x_fm[oc][:, tn * 512:(tn + 1) * 512],
                                                   ALU.add, ALU.add)
                    else:
                        i = v.tensor_tensor(x_fm[oc][:, tn * 512:(tn + 1) * 512],
                                            x_fm[oc][:, tn * 512:(tn + 1) * 512],
                                            ps[4 + oj % 2][:, :], ALU.add)
                    p.inc(i, "dve", label=f"{tag}_oev{hg}_{oj}")

            p.on(blk, "gpsimd", g_fn)
            p.on(blk, "tensor", t_fn)
            p.on(blk, "scalar", s_fn)
            p.on(blk, "vector", v_fn)

    def register_bprev():
        for hg in range(2):
            prev = None
            for c in range(4):
                for tn in range(NT):
                    for qk in range(2):
                        lab = f"at_bprev{hg}_{c}_{tn}_{qk}"
                        if prev is None:
                            p.marks[lab] = p.marks.get(f"at_bmm{1 - hg}_3_3_1", 0) if hg == 1 else 0
                        else:
                            p.marks[lab] = p.marks.get(f"at_bmm{hg}_{prev[0]}_{prev[1]}_{prev[2]}", 0)
                        prev = (c, tn, qk)

    def output_stage():
        with p.block() as blk:
            def sy_fn(sy):
                p.dwait(sy, "wload", label="prologue_loads")
                for c in range(HC):
                    i = sy.dma_start(y_out[c * 128:(c + 1) * 128, :], x_fm[c][:])
                    p.dinc(i, "store")
                if DEV_STAGE:
                    for c in range(HC):
                        p.dinc(sy.dma_start(xn_out[c * 128:(c + 1) * 128, :], xn[c][:]), "store")
                    if DEV_STAGE == "rt":
                        p.dinc(sy.dma_start(rw_out.tensor.ap(), rwT_sb[:]), "store")
                        p.dinc(sy.dma_start(dn_out, r_dn[:]), "store")
                        p.dinc(sy.dma_start(rcp_out, r_mj[:]), "store")
                        p.dinc(sy.dma_start(sco_out, r_scores[:]), "store")
                        p.dinc(sy.dma_start(sel_out, r_a[:]), "store")
                p.dwait(sy, "store")
            p.on(blk, "sync", sy_fn)

    def emit_all():
        prologue()
        if DO_FFN1:
            ln_stage(0)
            if DEV_STAGE != "ln":
                router_stage(0)
                if DEV_STAGE != "rt":
                    ffn_units(0)
        if DO_ATTN:
            ln_stage(2)
            attn_consts()
            attn_pass(0)
            attn_pass(1)
        if DO_FFN2:
            ln_stage(1)
            router_stage(1)
            ffn_units(1)
        output_stage()

    p.start(dry=True)
    emit_all()
    register_bprev()
    final_cnt = dict(p.cnt)
    p.start(dry=False)
    emit_all()
    assert final_cnt == p.cnt, (final_cnt, p.cnt)
    return nc



def _make_consts():
    import ml_dtypes
    j = np.arange(128) % 32
    ts = float(10000.0) ** (2.0 * j / HD)
    pos = np.arange(T)
    ang = pos[None, :] / ts[:, None]
    cos_t = np.cos(ang).astype(ml_dtypes.bfloat16)
    sin_t = np.sin(ang).astype(ml_dtypes.bfloat16)
    perm = np.zeros((128, 128), np.float32)
    for r in range(128):
        b = (r // 64) * 64
        jj = r - b
        if jj < 32:
            perm[b + jj + 32, r] = -1.0
        else:
            perm[b + jj - 32, r] = 1.0
    perm = perm.astype(ml_dtypes.bfloat16)
    masks = np.zeros((128, 4 * 512), np.float32)
    for ri, rel in enumerate((0, 128, 256, 384)):
        k = np.arange(128)[:, None] + rel
        q = np.arange(512)[None, :]
        masks[:, ri * 512:(ri + 1) * 512] = (k <= q).astype(np.float32)
    masks = masks.astype(ml_dtypes.bfloat16)
    iden = np.eye(128, dtype=np.float32)
    return cos_t, sin_t, perm, masks, iden


def _pad_gu(W, S):
    W = np.asarray(W, np.float32)
    S = np.asarray(S, np.float32)
    out = np.zeros((66, 768, D), np.float32)
    out[:64, :I_EXP, :] = W
    out[64, :768, :] = S[0:768]
    out[65, :640, :] = S[768:1408]
    return out


def _pad_d(W, S):
    W = np.asarray(W, np.float32)
    S = np.asarray(S, np.float32)
    out = np.zeros((66, D, 768), np.float32)
    out[:64, :, :I_EXP] = W
    out[64, :, :768] = S[:, 0:768]
    out[65, :, :640] = S[:, 768:1408]
    return out


def kernel(L0_attn_x, L0_attn_wq_w, L0_attn_wq_b, L0_attn_wk_w, L0_attn_wk_b,
           L0_attn_wv_w, L0_attn_wv_b, L0_attn_wo_w, L0_attn_wo_b,
           L0_ffn_gate_w, L0_ffn_gate_bias, L0_ffn_Wg, L0_ffn_Wu, L0_ffn_Wd,
           L0_ffn_Sg, L0_ffn_Su, L0_ffn_Sd):
    cos_t, sin_t, perm, masks, iden = _make_consts()
    nc = build_nc()
    common = dict(
        wq=np.ascontiguousarray(np.asarray(L0_attn_wq_w, np.float32)),
        wk=np.ascontiguousarray(np.asarray(L0_attn_wk_w, np.float32)),
        wv=np.ascontiguousarray(np.asarray(L0_attn_wv_w, np.float32)),
        wo=np.ascontiguousarray(np.asarray(L0_attn_wo_w, np.float32)),
        qb=np.asarray(L0_attn_wq_b, np.float32), kb=np.asarray(L0_attn_wk_b, np.float32),
        vb=np.asarray(L0_attn_wv_b, np.float32), ob=np.asarray(L0_attn_wo_b, np.float32),
        gate_w=np.ascontiguousarray(np.asarray(L0_ffn_gate_w, np.float32)),
        gate_b=np.asarray(L0_ffn_gate_bias, np.float32),
        Wg=_pad_gu(L0_ffn_Wg, L0_ffn_Sg), Wu=_pad_gu(L0_ffn_Wu, L0_ffn_Su),
        Wd=_pad_d(L0_ffn_Wd, L0_ffn_Sd),
        cos_t=cos_t, sin_t=sin_t, perm=perm, masks=masks, iden=iden,
    )
    x = np.asarray(L0_attn_x, np.float32)
    in_maps = []
    for c in range(8):
        m = dict(common)
        m["x_fm"] = np.ascontiguousarray(x[c].T)
        in_maps.append(m)
    res = run_bass_kernel_spmd(nc, in_maps, core_ids=list(range(8)))
    out = np.stack([np.asarray(r["y_fm"]).T for r in res.results], axis=0)
    return np.ascontiguousarray(out.astype(np.float32))
